# version 10
# baseline (speedup 1.0000x reference)
import os
import numpy as np
import ml_dtypes
from contextlib import ExitStack

_STAGE = int(os.environ.get("KSTAGE", "3"))
_KATT = os.environ.get("KATT", "full")
_NORACE = os.environ.get("KNORACE", "1") == "1"
_LOCAL = os.environ.get("KLOCAL", "0") == "1"

import concourse.bass as bass
import concourse.mybir as mybir
from concourse.bass_utils import run_bass_kernel_spmd

F32 = mybir.dt.float32
BF16 = mybir.dt.bfloat16
I32 = mybir.dt.int32
AF = mybir.ActivationFunctionType
OP = mybir.AluOpType
AXX = mybir.AxisListType.X

D = 1024
DC = 8
S = 8192
NHEAD = 16
HD = 64
E = 8
F = 3264
FC = 26
TOKS_CORE = 4096
QTOK = 2048
TC = 1024
NCH1 = 4
NCH2 = 2
BIG = 1.0e9
EPS = 1e-5
bfdt = ml_dtypes.bfloat16

ENGINES = ("sync", "gpsimd", "tensor", "vector", "scalar")


class Plan:
    def __init__(self):
        self.ops = {e: [] for e in ENGINES}
        self.sem_names = []
        self.counts = {}

    def sem(self, name):
        if name not in self.counts:
            self.sem_names.append(name)
            self.counts[name] = 0
        return name

    def inc(self, name, by):
        self.counts[name] += by
        return self.counts[name]

    def val(self, name):
        return self.counts[name]


class Ctx:
    def __init__(self, nc, plan):
        self.nc = nc
        self.plan = plan
        self.dr = {}
        self.st = {}
        self.regs = {}
        self.sems = {}
        self.peA_prev = None
        self.peB_prev = None

    def emit(self, engine, fn):
        self.plan.ops[engine].append(fn)

    def W(self, engine, sem, upto=None):
        v = self.plan.val(sem) if upto is None else upto
        def f(eng, cx):
            eng.wait_ge(cx.sems[sem], v)
        self.emit(engine, f)

    def I(self, engine, fn, sem, by):
        self.plan.inc(sem, by)
        def f(eng, cx):
            fn(eng, cx).then_inc(cx.sems[sem], by)
        self.emit(engine, f)

    def N(self, engine, fn):
        self.emit(engine, lambda eng, cx, fn=fn: fn(eng, cx))

    def CH(self, engine, fn, sem):
        v = self.plan.inc(sem, 1)
        def f(eng, cx):
            fn(eng, cx).then_inc(cx.sems[sem], 1)
            eng.wait_ge(cx.sems[sem], v)
        self.emit(engine, f)



def sub_ap(view, extra_off, pattern):
    if isinstance(view, bass.AP):
        return bass.AP(view.tensor, view.offset + extra_off, pattern)
    return bass.AP(view, extra_off, pattern)


def emit_ln(cx, src_name, dst_name):
    for dc in range(DC):
        cx.W("vector", "PE")
        cx.CH("vector", lambda e, c, dc=dc: e.tensor_copy(c.st["xb_s"][:, :], c.st[src_name][:, dc, :]), "DV")
        cx.W("scalar", "PE")
        cx.W("scalar", "DV")
        cx.I("scalar", lambda e, c, dc=dc: e.activation(c.st["xsq_s"][:, :], c.st[src_name][:, dc, :], AF.Square), "SC", 1)
        cx.W("tensor", "DV")
        cx.W("tensor", "SC")
        for th in range(2):
            sl = slice(th * 512, (th + 1) * 512)
            cx.N("tensor", lambda e, c, dc=dc, sl=sl, th=th: e.matmul(
                c.st["p%d" % th][0:1, :], c.st["onesc_sb"][:, 0:1], c.st["xb_s"][:, sl],
                start=(dc == 0), stop=(dc == DC - 1)))
            emitter = cx.N if th == 0 else (lambda eng, fn: cx.I(eng, fn, "PE", 1))
            emitter("tensor", lambda e, c, dc=dc, sl=sl, th=th: e.matmul(
                c.st["p%d" % (2 + th)][0:1, :], c.st["onesc_sb"][:, 0:1], c.st["xsq_s"][:, sl],
                start=(dc == 0), stop=(dc == DC - 1)))
    for th in range(2):
        sl = slice(th * 512, (th + 1) * 512)
        cx.W("vector", "PE")
        cx.CH("vector", lambda e, c, th=th: e.tensor_scalar(c.st["lnA"][0:1, 0:512], c.st["p%d" % th][0:1, :], 1.0 / D, None, OP.mult), "DV")
        cx.CH("vector", lambda e, c, th=th: e.tensor_scalar(c.st["lnA"][0:1, 512:1024], c.st["p%d" % (2 + th)][0:1, :], 1.0 / D, None, OP.mult), "DV")
        cx.CH("vector", lambda e, c: e.tensor_mul(c.st["lnB"][0:1, 0:512], c.st["lnA"][0:1, 0:512], c.st["lnA"][0:1, 0:512]), "DV")
        cx.CH("vector", lambda e, c: e.tensor_sub(c.st["lnB"][0:1, 0:512], c.st["lnA"][0:1, 512:1024], c.st["lnB"][0:1, 0:512]), "DV")
        cx.CH("vector", lambda e, c: e.tensor_scalar(c.st["lnB"][0:1, 0:512], c.st["lnB"][0:1, 0:512], EPS, None, OP.add), "DV")
        cx.W("scalar", "DV")
        cx.I("scalar", lambda e, c: e.activation(c.st["lnB"][0:1, 512:1024], c.st["lnB"][0:1, 0:512], AF.Sqrt), "SC", 1)
        cx.W("vector", "SC")
        cx.CH("vector", lambda e, c: e.reciprocal(c.st["lnC"][0:1, 0:512], c.st["lnB"][0:1, 512:1024]), "DV")
        cx.CH("vector", lambda e, c: e.tensor_mul(c.st["lnC"][0:1, 512:1024], c.st["lnA"][0:1, 0:512], c.st["lnC"][0:1, 0:512]), "DV")
        cx.CH("vector", lambda e, c: e.tensor_copy(c.st["lnb"][0:1, :], c.st["lnC"][0:1, :]), "DV")
        cx.W("tensor", "DV")
        pr = "p%d" % th
        pn = "p%d" % (2 + th)
        cx.N("tensor", lambda e, c, pr=pr: e.matmul(c.st[pr][:, :], c.st["onesl_sb"][0:1, :], c.st["lnb"][0:1, 0:512], start=True, stop=True))
        cx.I("tensor", lambda e, c, pn=pn: e.matmul(c.st[pn][:, :], c.st["onesl_sb"][0:1, :], c.st["lnb"][0:1, 512:1024], start=True, stop=True), "PE", 1)
        cx.W("vector", "PE")
        for dc in range(DC):
            cx.CH("vector", lambda e, c, dc=dc, sl=sl, pn=pn: e.tensor_tensor(
                c.st["lnt"][:, :], c.st[src_name][:, dc, sl], c.st[pn][:, :], OP.subtract), "DV")
            cx.CH("vector", lambda e, c, dc=dc, sl=sl, pr=pr: e.tensor_tensor(
                c.st[dst_name][:, dc, sl], c.st["lnt"][:, :], c.st[pr][:, :], OP.mult), "DV")


def emit_gate(cx):
    for th in range(2):
        sl = slice(th * 512, (th + 1) * 512)
        ps = "p4" if th == 0 else "p5"
        cx.W("tensor", "DV")
        for dc in range(DC):
            emitter = cx.N if dc < DC - 1 else (lambda eng, fn: cx.I(eng, fn, "PE", 1))
            emitter("tensor", lambda e, c, dc=dc, sl=sl, ps=ps: e.matmul(
                c.st[ps][0:8, :], c.st["gw_sb"][:, dc * 8:(dc + 1) * 8], c.st["xn_ch"][:, dc, sl],
                start=(dc == 0), stop=(dc == DC - 1)))
        cx.W("scalar", "PE")
        cx.I("scalar", lambda e, c, th=th, ps=ps: e.activation(
            c.st["gsc"][0:8, th * 512:(th + 1) * 512], c.st[ps][0:8, :], AF.Identity,
            bias=c.st["gb_sb"][0:8, 0:1]), "SC", 1)
    cx.W("tensor", "SC")
    for i in range(8):
        emitter = cx.N if i < 7 else (lambda eng, fn: cx.I(eng, fn, "PE", 1))
        emitter("tensor", lambda e, c, i=i: e.transpose(
            c.st["p6"][:, i * 8:(i + 1) * 8], c.st["gsc"][0:8, i * 128:(i + 1) * 128], c.st["ident_sb"][0:8, 0:8]))
    cx.W("vector", "PE")
    cx.CH("vector", lambda e, c: e.tensor_copy(c.st["gtm"][:, :], c.st["p6"][:, 0:64]), "DV")
    for i in range(8):
        tsl = slice(i * 8, (i + 1) * 8)
        cx.CH("vector", lambda e, c, tsl=tsl, i=i: e.reduce_max(c.st["gm1"][:, i:i + 1], c.st["gtm"][:, tsl], axis=AXX), "DV")
        cx.CH("vector", lambda e, c, tsl=tsl, i=i: e.tensor_scalar(c.st["gtmp"][:, :], c.st["gtm"][:, tsl], c.st["gm1"][:, i:i + 1], None, OP.is_equal), "DV")
        cx.CH("vector", lambda e, c: e.tensor_scalar(c.st["gtmp"][:, :], c.st["gtmp"][:, :], -BIG, BIG, OP.mult, OP.add), "DV")
        cx.CH("vector", lambda e, c: e.tensor_tensor(c.st["gtmp"][:, :], c.st["gtmp"][:, :], c.st["iota_sb"][:, :], OP.add), "DV")
        cx.CH("vector", lambda e, c, i=i: e.tensor_reduce(c.st["gi"][:, i:i + 1], c.st["gtmp"][:, :], axis=AXX, op=OP.min), "DV")
        cx.CH("vector", lambda e, c, tsl=tsl, i=i: e.tensor_scalar(c.st["gmask1"][:, tsl], c.st["iota_sb"][:, :], c.st["gi"][:, i:i + 1], None, OP.is_equal), "DV")
        cx.CH("vector", lambda e, c, tsl=tsl: e.scalar_tensor_tensor(c.st["gtmp"][:, :], c.st["gmask1"][:, tsl], -BIG, c.st["gtm"][:, tsl], OP.mult, OP.add), "DV")
        cx.CH("vector", lambda e, c, i=i: e.reduce_max(c.st["gm2"][:, i:i + 1], c.st["gtmp"][:, :], axis=AXX), "DV")
        cx.CH("vector", lambda e, c, i=i: e.tensor_scalar(c.st["gtmp"][:, :], c.st["gtmp"][:, :], c.st["gm2"][:, i:i + 1], None, OP.is_equal), "DV")
        cx.CH("vector", lambda e, c: e.tensor_scalar(c.st["gtmp"][:, :], c.st["gtmp"][:, :], -BIG, BIG, OP.mult, OP.add), "DV")
        cx.CH("vector", lambda e, c: e.tensor_tensor(c.st["gtmp"][:, :], c.st["gtmp"][:, :], c.st["iota_sb"][:, :], OP.add), "DV")
        cx.CH("vector", lambda e, c, i=i: e.tensor_reduce(c.st["gi"][:, i:i + 1], c.st["gtmp"][:, :], axis=AXX, op=OP.min), "DV")
        cx.CH("vector", lambda e, c, tsl=tsl, i=i: e.tensor_scalar(c.st["gmask2"][:, tsl], c.st["iota_sb"][:, :], c.st["gi"][:, i:i + 1], None, OP.is_equal), "DV")
        cx.CH("vector", lambda e, c, i=i: e.tensor_sub(c.st["gd21"][:, i:i + 1], c.st["gm2"][:, i:i + 1], c.st["gm1"][:, i:i + 1]), "DV")
    cx.W("scalar", "DV")
    cx.I("scalar", lambda e, c: e.activation(c.st["gw2"][:, :], c.st["gd21"][:, :], AF.Sigmoid), "SC", 1)
    cx.W("vector", "SC")
    cx.CH("vector", lambda e, c: e.tensor_scalar(c.st["gw1"][:, :], c.st["gw2"][:, :], -1.0, 1.0, OP.mult, OP.add), "DV")
    for i in range(8):
        tsl = slice(i * 8, (i + 1) * 8)
        cx.CH("vector", lambda e, c, tsl=tsl, i=i: e.tensor_scalar(c.st["gmask1"][:, tsl], c.st["gmask1"][:, tsl], c.st["gw1"][:, i:i + 1], None, OP.mult), "DV")
        cx.CH("vector", lambda e, c, tsl=tsl, i=i: e.tensor_scalar(c.st["gmask2"][:, tsl], c.st["gmask2"][:, tsl], c.st["gw2"][:, i:i + 1], None, OP.mult), "DV")
        cx.CH("vector", lambda e, c, tsl=tsl: e.tensor_add(c.st["gcomb"][:, tsl], c.st["gmask1"][:, tsl], c.st["gmask2"][:, tsl]), "DV")
    cx.W("tensor", "DV")
    for i in range(8):
        ps = "p6" if i < 4 else "p7"
        off = (i % 4) * 128
        emitter = cx.N if i < 7 else (lambda eng, fn: cx.I(eng, fn, "PE", 1))
        emitter("tensor", lambda e, c, i=i, ps=ps, off=off: e.transpose(
            c.st[ps][0:8, off:off + 128], c.st["gcomb"][:, i * 8:(i + 1) * 8], c.st["ident_sb"][:, :]))
    cx.W("vector", "PE")
    cx.CH("vector", lambda e, c: e.tensor_scalar(c.st["combh"][0:8, 0:512], c.st["p6"][0:8, :], 0.5, None, OP.mult), "DV")
    cx.CH("vector", lambda e, c: e.tensor_scalar(c.st["combh"][0:8, 512:1024], c.st["p7"][0:8, :], 0.5, None, OP.mult), "DV")


def emit_moe(cx):
    plan = cx.plan
    for e_i in range(E):
        cx.W("tensor", "DV")
        cx.W("tensor", "SC")
        cx.N("tensor", lambda e, c, e_i=e_i: e.matmul(
            c.st["p4"][:, :], c.st["sel8_sb"][0:8, e_i * 128:(e_i + 1) * 128],
            c.st["combh"][0:8, 0:512], start=True, stop=True))
        cx.I("tensor", lambda e, c, e_i=e_i: e.matmul(
            c.st["p5"][:, :], c.st["sel8_sb"][0:8, e_i * 128:(e_i + 1) * 128],
            c.st["combh"][0:8, 512:1024], start=True, stop=True), "PE", 1)
        cx.W("vector", "PE")
        cx.CH("vector", lambda e, c: e.tensor_copy(c.st["wbc"][:, 0:512], c.st["p4"][:, :]), "DV")
        cx.CH("vector", lambda e, c: e.tensor_copy(c.st["wbc"][:, 512:1024], c.st["p5"][:, :]), "DV")

        pe_after = {}
        dv_after = {}
        for f in range(FC):
            buf = ("wu_a", "wu_b", "wu_c")[f % 3]
            pus = ("p0", "p2") if f % 2 == 0 else ("p4", "p6")
            pgs = ("p1", "p3") if f % 2 == 0 else ("p5", "p7")
            if f >= 3:
                cx.W("gpsimd", "PE", pe_after[f - 3])
            elif cx.peA_prev is not None:
                cx.W("gpsimd", "PE", cx.peA_prev)
            else:
                cx.W("gpsimd", "PE")
            wsem = ("WGTA", "WGTB", "WGTC")[f % 3]
            cx.I("gpsimd", lambda e, c, e_i=e_i, f=f, buf=buf: e.dma_start(
                out=c.st[buf][:, :], in_=c.dr["upgp"].ap()[e_i, f]), wsem, 16)
            cx.W("tensor", wsem)
            if f >= 2:
                cx.W("tensor", "DV", dv_after[f - 2])
            else:
                cx.W("tensor", "DV")
            for dc in range(DC):
                for th in range(2):
                    cx.N("tensor", lambda e, c, dc=dc, th=th, buf=buf, pus=pus: e.matmul(
                        c.st[pus[th]][:, :], c.st[buf][:, dc * 256:dc * 256 + 128],
                        c.st["xn_ch"][:, dc, th * 512:(th + 1) * 512],
                        start=(dc == 0), stop=(dc == DC - 1)))
            for dc in range(DC):
                for th in range(2):
                    emitter = cx.N if not (th == 1 and dc == DC - 1) else (lambda eng, fn: cx.I(eng, fn, "PE", 1))
                    emitter("tensor", lambda e, c, dc=dc, th=th, buf=buf, pgs=pgs: e.matmul(
                        c.st[pgs[th]][:, :], c.st[buf][:, dc * 256 + 128:dc * 256 + 256],
                        c.st["xn_ch"][:, dc, th * 512:(th + 1) * 512],
                        start=(dc == 0), stop=(dc == DC - 1)))
            pe_after[f] = plan.val("PE")
            cx.W("scalar", "PE")
            cx.W("scalar", "DV")
            cx.W("vector", "PE")
            for th in range(2):
                tsl = slice(th * 512, (th + 1) * 512)
                bcol = e_i * FC + f
                cx.I("scalar", lambda e, c, pus=pus, th=th, tsl=tsl, bcol=bcol: e.activation(
                    c.st["upev"][:, tsl], c.st[pus[th]][:, :], AF.Identity,
                    bias=c.st["upb_sb"][:, bcol:bcol + 1]), "SC", 1)
                cx.CH("vector", lambda e, c, pgs=pgs, th=th, tsl=tsl, bcol=bcol: e.tensor_scalar(
                    c.st["gpev"][:, tsl], c.st[pgs[th]][:, :], c.st["gpb_sb"][:, bcol:bcol + 1], None, OP.add), "DV")
                cx.W("vector", "SC")
                cx.CH("vector", lambda e, c, tsl=tsl: e.tensor_tensor(
                    c.st["upev"][:, tsl], c.st["upev"][:, tsl], c.st["gpev"][:, tsl], OP.mult), "DV")
                cx.W("scalar", "DV")
                cx.I("scalar", lambda e, c, tsl=tsl: e.activation(
                    c.st["hsil"][:, tsl], c.st["upev"][:, tsl], AF.Silu), "SC", 1)
                cx.W("vector", "SC")
                cx.I("vector", lambda e, c, f=f, tsl=tsl: e.tensor_tensor(
                    c.st["hprime"][:, f, tsl], c.st["hsil"][:, tsl], c.st["wbc"][:, tsl], OP.mult), "DV", 1)
            dv_after[f] = plan.val("DV")
        cx.peA_prev = pe_after[FC - 1]

        for th in range(2):
            tsl = slice(th * 512, (th + 1) * 512)
            pe_b = {}
            for f in range(FC):
                buf = "wd_a" if f % 2 == 0 else "wd_b"
                if f >= 2:
                    cx.W("gpsimd", "PE", pe_b[f - 2])
                elif cx.peB_prev is not None:
                    cx.W("gpsimd", "PE", cx.peB_prev)
                else:
                    cx.W("gpsimd", "PE")
                wsem = "WGTDA" if f % 2 == 0 else "WGTDB"
                cx.I("gpsimd", lambda e, c, e_i=e_i, f=f, buf=buf: e.dma_start(
                    out=c.st[buf][:, :], in_=c.dr["dwn"].ap()[e_i, f]), wsem, 16)
                cx.W("tensor", wsem)
                if f == 0:
                    cx.W("tensor", "DV")
                last = (f == FC - 1) and (e_i != E - 1)
                for dd in range(DC):
                    emitter = cx.N if dd < DC - 1 else (lambda eng, fn: cx.I(eng, fn, "PE", 1))
                    emitter("tensor", lambda e, c, f=f, dd=dd, tsl=tsl, buf=buf, last=last: e.matmul(
                        c.st["p%d" % dd][:, :], c.st[buf][:, dd * 128:(dd + 1) * 128],
                        c.st["hprime"][:, f, tsl],
                        start=(f == 0), stop=last))
                pe_b[f] = plan.val("PE")
            cx.peB_prev = pe_b[FC - 1]
            if e_i == E - 1:
                for dd in range(DC):
                    emitter = cx.N if dd < DC - 1 else (lambda eng, fn: cx.I(eng, fn, "PE", 1))
                    emitter("tensor", lambda e, c, dd=dd, tsl=tsl: e.matmul(
                        c.st["p%d" % dd][:, :], c.st["dbn_sb"][0:8, dd * 128:(dd + 1) * 128],
                        c.st["combh"][0:8, tsl], start=False, stop=True))
            cx.W("vector", "PE")
            for dd in range(DC):
                cx.CH("vector", lambda e, c, dd=dd, tsl=tsl: e.tensor_tensor(
                    c.st["x_ch"][:, dd, tsl], c.st["x_ch"][:, dd, tsl], c.st["p%d" % dd][:, :], OP.add), "DV")


def emit_barrier(cx, bar_idx):
    st_needed = cx.plan.val("ST")
    cx.W("gpsimd", "ST", st_needed)
    b = bar_idx - 1

    def wr_flag(eng, cx_):
        r = cx_.regs["gpsimd"]
        dst = bass.AP(cx_.flg, r["rFmB%d" % b], [[4, 1], [1, 4]])
        return eng.dma_start(out=dst, in_=cx_.st["flagv_sb"][b:b + 1, 0:4])
    cx.I("gpsimd", wr_flag, "GPD", 16)
    cx.W("gpsimd", "GPD", cx.plan.val("GPD"))

    def poll(eng, cx_):
        r = cx_.regs["gpsimd"]
        eng.reg_load(r["rEx"], cx_.st["flagv_sb"][b:b + 1, 0:1])
        src = bass.AP(cx_.flg, r["rFpB%d" % b], [[4, 1], [1, 4]])
        eng.reg_mov(r["rC"], 1)
        with eng.While(r["rC"]):
            eng.dma_start(out=cx_.st["flgrd_sb"][0:1, 0:4], in_=src).then_inc(cx_.sems["PF"], 16)
            eng.reg_add(r["rW"], r["rW"], 16)
            eng.wait_ge(cx_.sems["PF"], r["rW"])
            eng.reg_load(r["rV"], cx_.st["flgrd_sb"][0:1, 0:1])
            eng.reg_alu(r["rC"], r["rV"], r["rEx"], OP.subtract)
    if not _LOCAL:
        cx.N("gpsimd", poll)
    cx.I("gpsimd", lambda e, c: e.nop(), "BAR", 1)


def emit_attention(cx):
    plan = cx.plan
    for hp in range(4):
        if hp > 0:
            cx.W("scalar", "ST", plan.val("ST"))
            cx.W("sync", "ST", plan.val("ST"))
        pe_kb = {}
        for kb in range(32):
            buf = "xnt_a" if kb % 2 == 0 else "xnt_b"
            if kb >= 2:
                cx.W("sync", "PE", pe_kb[kb - 2])
            ARP = 26 * 1024
            def f_load0(eng, c, kb=kb, buf=buf):
                src = bass.AP(c.xn1g, kb * 128, [[S, 128], [128 * S, DC], [1, 128]])
                dst = sub_ap(c.st[buf], 0, [[ARP, 128], [256, DC], [1, 128]])
                return eng.dma_start(out=dst, in_=src)
            lsem = "LDA" if kb % 2 == 0 else "LDB"
            cx.I("sync", f_load0, lsem, 16)
            def f_load1(eng, c, kb=kb, buf=buf):
                src = bass.AP(c.xn1g, 4096 + kb * 128, [[S, 128], [128 * S, DC], [1, 128]])
                dst = sub_ap(c.st[buf], 128, [[ARP, 128], [256, DC], [1, 128]])
                return eng.dma_start(out=dst, in_=src)
            cx.I("sync", f_load1, lsem, 16)
            cx.W("tensor", lsem)
            cx.W("tensor", "SC")
            cx.W("tensor", "DV")
            for dc in range(DC):
                cx.N("tensor", lambda e, c, dc=dc, hp=hp, buf=buf: e.matmul(
                    c.st["p0"][:, 0:256], c.st["wk_sb"][:, dc * 512 + hp * 128:dc * 512 + (hp + 1) * 128],
                    c.st[buf][:, dc * 256:(dc + 1) * 256],
                    start=(dc == 0), stop=(dc == DC - 1)))
            for tsub in range(2):
                for dc in range(DC):
                    emitter = cx.N if not (tsub == 1 and dc == DC - 1) else (lambda eng, fn: cx.I(eng, fn, "PE", 1))
                    emitter("tensor", lambda e, c, dc=dc, tsub=tsub, hp=hp, buf=buf: e.matmul(
                        c.st["p1"][:, tsub * 128:(tsub + 1) * 128],
                        c.st[buf][:, dc * 256 + tsub * 128:dc * 256 + (tsub + 1) * 128],
                        c.st["wv_sb"][:, dc * 512 + hp * 128:dc * 512 + (hp + 1) * 128],
                        start=(dc == 0), stop=(dc == DC - 1)))
            pe_kb[kb] = plan.val("PE")
            cx.W("scalar", "PE")
            cx.I("scalar", lambda e, c, kb=kb, hp=hp: e.activation(
                c.st["k_sb"][:, kb * 256:(kb + 1) * 256], c.st["p0"][:, 0:256], AF.Identity,
                bias=c.st["bqkv_sb"][:, hp * 3 + 1:hp * 3 + 2]), "SC", 1)
            cx.W("vector", "PE")
            cx.I("vector", lambda e, c, kb=kb: e.tensor_copy(
                c.st["v_sb"][:, kb * 256:(kb + 1) * 256], c.st["p1"][:, 0:256]), "DV", 1)
        pe_qb = {}
        for qb in range(16):
            buf = "xnt_a" if qb % 2 == 0 else "xnt_b"
            if qb >= 2:
                cx.W("sync", "PE", pe_qb[qb - 2])
            else:
                cx.W("sync", "PE", pe_kb[31 - (1 - qb)])
            def f_loadq(eng, c, qb=qb, buf=buf):
                src = bass.AP(c.xn1p, qb * 256, [[4096, 128], [128 * 4096, DC], [1, 256]])
                return eng.dma_start(out=c.st[buf][:, :], in_=src)
            lsem = "LDA" if qb % 2 == 0 else "LDB"
            cx.I("sync", f_loadq, lsem, 16)
            cx.W("tensor", lsem)
            cx.W("tensor", "SC")
            for dc in range(DC):
                emitter = cx.N if dc < DC - 1 else (lambda eng, fn: cx.I(eng, fn, "PE", 1))
                emitter("tensor", lambda e, c, dc=dc, hp=hp, buf=buf: e.matmul(
                    c.st["p0"][:, 0:256], c.st["wq_sb"][:, dc * 512 + hp * 128:dc * 512 + (hp + 1) * 128],
                    c.st[buf][:, dc * 256:(dc + 1) * 256],
                    start=(dc == 0), stop=(dc == DC - 1)))
            pe_qb[qb] = plan.val("PE")
            cx.W("scalar", "PE")
            cx.I("scalar", lambda e, c, qb=qb, hp=hp: e.activation(
                c.st["q_sb"][:, qb * 256:(qb + 1) * 256], c.st["p0"][:, 0:256], AF.Identity,
                bias=c.st["bqkv8_sb"][:, hp * 3:hp * 3 + 1], scale=0.125), "SC", 1)
        if _KATT == "proj":
            continue
        for cq in range(16):
            n_kt = 4 * cq + 4
            sc_after = {}
            dv_after_kt = {}
            pe_scores = {}
            sc_boundary = cx.plan.val("SC")
            dv_boundary = cx.plan.val("DV")

            def emit_scores(kt, cq=cq):
                plg0 = "p2" if kt % 2 == 0 else "p3"
                plg1 = "p0" if kt % 2 == 0 else "p1"
                if kt >= 2:
                    cx.W("tensor", "SC", sc_after[kt - 2])
                else:
                    cx.W("tensor", "SC", sc_boundary)
                    if cq == 0:
                        cx.W("tensor", "DV", dv_boundary)
                for hh in range(2):
                    plg = plg0 if hh == 0 else plg1
                    emitter = cx.N if hh == 0 else (lambda eng, fn: cx.I(eng, fn, "PE", 1))
                    emitter("tensor", lambda e, c, kt=kt, cq=cq, hh=hh, plg=plg: e.matmul(
                        c.st[plg][:, 0:256],
                        c.st["k_sb"][hh * 64:(hh + 1) * 64, kt * 128:(kt + 1) * 128],
                        c.st["q_sb"][hh * 64:(hh + 1) * 64, cq * 256:(cq + 1) * 256],
                        start=True, stop=True, tile_position=(hh * 64, 0)))
                pe_scores[kt] = cx.plan.val("PE")

            def emit_exp_mask(kt, cq=cq):
                plg0 = "p2" if kt % 2 == 0 else "p3"
                plg1 = "p0" if kt % 2 == 0 else "p1"
                pt = "pt_a" if kt % 2 == 0 else "pt_b"
                cx.W("scalar", "PE", pe_scores[kt])
                cx.I("scalar", lambda e, c, plg0=plg0, pt=pt: e.activation(
                    c.st[pt][:, 0:256], c.st[plg0][:, 0:256], AF.Exp), "SC", 1)
                cx.I("scalar", lambda e, c, plg1=plg1, pt=pt: e.activation(
                    c.st[pt][:, 256:512], c.st[plg1][:, 0:256], AF.Exp), "SC", 1)
                sc_after[kt] = cx.plan.val("SC")
                r = kt - 4 * cq
                if r >= 0:
                    cx.W("vector", "SC", sc_after[kt])
                    cx.I("vector", lambda e, c, pt=pt, r=r: e.tensor_tensor(
                        c.st[pt][:, :], c.st[pt][:, :], c.st["mask_sb"][:, r * 512:(r + 1) * 512], OP.mult), "DV", 1)
                dv_after_kt[kt] = cx.plan.val("DV")

            def emit_av(kt, cq=cq, n_kt=n_kt):
                pt = "pt_a" if kt % 2 == 0 else "pt_b"
                r = kt - 4 * cq
                if r >= 0:
                    cx.W("tensor", "DV", dv_after_kt[kt])
                else:
                    cx.W("tensor", "SC", sc_after[kt])
                for hh in range(2):
                    pav = "p4" if hh == 0 else "p5"
                    cx.N("tensor", lambda e, c, kt=kt, hh=hh, pt=pt, n_kt=n_kt, pav=pav: e.matmul(
                        c.st[pav][hh * 64:(hh + 1) * 64, 0:256],
                        c.st["v_sb"][:, kt * 128 + hh * 64:kt * 128 + (hh + 1) * 64],
                        c.st[pt][:, hh * 256:(hh + 1) * 256],
                        start=(kt == 0), stop=(kt == n_kt - 1), tile_position=(0, hh * 64)))
                cx.I("tensor", lambda e, c, kt=kt, pt=pt, n_kt=n_kt: e.matmul(
                    c.st["p6"][0:1, 0:512],
                    c.st["onesc_sb"][:, 0:1],
                    c.st[pt][:, 0:512],
                    start=(kt == 0), stop=(kt == n_kt - 1)), "PE", 1)

            if _KATT == "lg":
                for kt in range(n_kt):
                    emit_scores(kt)
                    emit_exp_mask(kt)
                continue
            emit_scores(0)
            emit_exp_mask(0)
            for kt in range(n_kt):
                if kt + 1 < n_kt:
                    emit_scores(kt + 1)
                    emit_exp_mask(kt + 1)
                emit_av(kt)
            if _KATT == "av":
                continue
            cx.W("vector", "PE")
            cx.CH("vector", lambda e, c: e.tensor_copy(c.st["rden"][0:1, :], c.st["p6"][0:1, :]), "DV")
            cx.CH("vector", lambda e, c: e.reciprocal(c.st["rden"][0:1, :], c.st["rden"][0:1, :]), "DV")
            cx.CH("vector", lambda e, c: e.tensor_copy(c.st["rdenb"][0:1, :], c.st["rden"][0:1, :]), "DV")
            cx.W("tensor", "DV")
            for hh in range(2):
                emitter = cx.N if hh == 0 else (lambda eng, fn: cx.I(eng, fn, "PE", 1))
                emitter("tensor", lambda e, c, hh=hh: e.matmul(
                    c.st["p7"][hh * 64:(hh + 1) * 64, 0:256],
                    c.st["onesl_sb"][0:1, 0:64],
                    c.st["rdenb"][0:1, hh * 256:(hh + 1) * 256],
                    start=True, stop=True, tile_position=(0, hh * 64)))
            cx.W("vector", "PE")
            cx.CH("vector", lambda e, c: e.tensor_copy(c.st["rbc"][:, :], c.st["p7"][:, 0:256]), "DV")
            cx.CH("vector", lambda e, c: e.tensor_tensor(c.st["avn"][0:64, :], c.st["p4"][0:64, 0:256], c.st["rbc"][0:64, :], OP.mult), "DV")
            cx.CH("vector", lambda e, c: e.tensor_tensor(c.st["avn"][64:128, :], c.st["p5"][64:128, 0:256], c.st["rbc"][64:128, :], OP.mult), "DV")
            cx.W("scalar", "DV")
            cx.I("scalar", lambda e, c, hp=hp, cq=cq: e.activation(
                c.st["att_hp"][:, cq * 256:(cq + 1) * 256], c.st["avn"][:, :], AF.Identity,
                bias=c.st["bqkv_sb"][:, hp * 3 + 2:hp * 3 + 3]), "SC", 1)
        cx.W("sync", "SC")
        def f_store(eng, c, hp=hp):
            r = c.regs["sync"]
            eng.reg_alu(r["rU"], r["rA"], hp * 128 * 4096, OP.add)
            dst = bass.AP(c.attnx, r["rU"], [[4096, 128], [1, 4096]])
            return eng.dma_start(out=dst, in_=c.st["att_hp"][:, :])
        cx.I("sync", f_store, "ST", 16)


def emit_outproj_ffn2(cx):
    for c2 in range(NCH2):
        for ocs in range(2):
            oc = c2 * 2 + ocs
            cx.W("sync", "PE")
            cx.W("sync", "ST", cx.plan.val("ST"))
            def f_la(eng, c, oc=oc):
                r = c.regs["sync"]
                eng.reg_alu(r["rU"], r["rO"], oc * 512, OP.add)
                src = bass.AP(c.attnx, r["rU"], [[4096, 128], [128 * 4096, DC], [1, 512]])
                return eng.dma_start(out=c.st["aall"][:, :, :], in_=src)
            cx.I("sync", f_la, "LDAL", 16)
            def f_lx(eng, c, oc=oc, ocs=ocs):
                src = bass.AP(c.x1park, oc * 512, [[QTOK, 128], [128 * QTOK, DC], [1, 512]])
                return eng.dma_start(out=c.st["x_ch"][:, :, ocs * 512:(ocs + 1) * 512], in_=src)
            cx.I("sync", f_lx, "LDX", 16)
            cx.W("tensor", "LDAL")
            cx.W("vector", "LDX")
            for dd in range(DC):
                ps = "p%d" % (dd % 2)
                cx.W("tensor", "SC")
                for cc in range(DC):
                    emitter = cx.N if cc < DC - 1 else (lambda eng, fn: cx.I(eng, fn, "PE", 1))
                    emitter("tensor", lambda e, c, dd=dd, cc=cc, ps=ps: e.matmul(
                        c.st[ps][:, :], c.st["wo_sb"][:, cc * 1024 + dd * 128:cc * 1024 + (dd + 1) * 128],
                        c.st["aall"][:, cc, :], start=(cc == 0), stop=(cc == DC - 1)))
                cx.W("scalar", "PE")
                cx.W("scalar", "DV")
                cx.I("scalar", lambda e, c, dd=dd, ps=ps: e.activation(
                    c.st["x2t"][:, :], c.st[ps][:, :], AF.Identity,
                    bias=c.st["bout_sb"][:, dd:dd + 1]), "SC", 1)
                cx.W("vector", "SC")
                cx.I("vector", lambda e, c, dd=dd, ocs=ocs: e.tensor_tensor(
                    c.st["x_ch"][:, dd, ocs * 512:(ocs + 1) * 512],
                    c.st["x_ch"][:, dd, ocs * 512:(ocs + 1) * 512],
                    c.st["x2t"][:, :], OP.add), "DV", 1)
        emit_ln(cx, "x_ch", "xn_ch")
        emit_gate(cx)
        emit_moe(cx)
        cx.W("sync", "DV")
        def f_out(eng, c, c2=c2):
            dst = c.dr["out"].ap()[:, c2 * TC:(c2 + 1) * TC].rearrange("(dc p) t -> p dc t", p=128)
            return eng.dma_start(out=dst, in_=c.st["x_ch"][:, :, :])
        cx.I("sync", f_out, "ST", 16)


def emit_ffn1(cx):
    for ch in range(NCH1):
        cx.W("sync", "DV")
        cx.W("sync", "ST", cx.plan.val("ST"))
        def f_lx(eng, c, ch=ch):
            src = bass.AP(c.dr["xin"], ch * TC, [[TOKS_CORE, 128], [128 * TOKS_CORE, DC], [1, TC]])
            return eng.dma_start(out=c.st["x_ch"][:, :, :], in_=src)
        cx.I("sync", f_lx, "LDX", 16)
        ld_now = cx.plan.val("LDX")
        st_now = cx.plan.val("ST")
        for engn in ("tensor", "vector", "scalar"):
            cx.W(engn, "LDX", ld_now)
        cx.W("vector", "ST", st_now)
        emit_ln(cx, "x_ch", "xn_ch")
        emit_gate(cx)
        emit_moe(cx)
        emit_ln(cx, "x_ch", "xn_ch")
        dv_now = cx.plan.val("DV")
        cx.W("sync", "DV", dv_now)
        if ch < 2:
            def f_park(eng, c, ch=ch):
                dst = bass.AP(c.x1park, ch * TC, [[QTOK, 128], [128 * QTOK, DC], [1, TC]])
                return eng.dma_start(out=dst, in_=c.st["x_ch"][:, :, :])
            cx.I("sync", f_park, "ST", 16)

            def f_packed(eng, c, ch=ch):
                r = c.regs["sync"]
                eng.reg_alu(r["rT"], r["rP"], ch * TC, OP.add)
                dst = bass.AP(c.xn1p, r["rT"], [[4096, 128], [128 * 4096, DC], [1, TC]])
                return eng.dma_start(out=dst, in_=c.st["xn_ch"][:, :, :])
            cx.I("sync", f_packed, "ST", 16)
        def f_g(eng, c, ch=ch):
            r = c.regs["sync"]
            eng.reg_load(r["rT"], c.st["offs_sb"][0:1, 6 + ch:7 + ch])
            dst = bass.AP(c.xn1g, r["rT"], [[S, 128], [128 * S, DC], [1, TC]])
            return eng.dma_start(out=dst, in_=c.st["xn_ch"][:, :, :])
        cx.I("sync", f_g, "ST", 16)



def build_nc():
    nc = bass.Bass("TRN2", target_bir_lowering=False, debug=False,
                   detect_race_conditions=not _NORACE)
    plan = Plan()
    cx = Ctx(nc, plan)

    def din(name, shape, dt=F32):
        cx.dr[name] = nc.dram_tensor(name, list(shape), dt, kind="ExternalInput")

    din("xin", (D, TOKS_CORE))
    din("wq", (D, 512)); din("wk", (D, 512)); din("wv", (D, 512))
    din("bqkv", (128, 12))
    din("wout", (D, D))
    din("bout", (128, DC))
    din("gwT", (D, E))
    din("gb", (E, 1))
    din("upgp", (E, FC, 128, DC * 256))
    din("dwn", (E, FC, 128, D))
    din("upb", (128, E * FC)); din("gpb", (128, E * FC))
    din("dbn", (E, D))
    din("maskc", (128, 4 * 512), BF16)
    din("iota8", (128, E))
    din("ident", (128, 128))
    din("onesc", (128, 1), BF16)
    din("onesl", (1, 128), BF16)
    din("sel8", (E, E * 128), BF16)
    din("offs", (1, 64), I32)
    din("flagv", (2, 16), I32)
    cx.dr["out"] = nc.dram_tensor("out", [D, QTOK], F32, kind="ExternalOutput")

    _spc = "Local" if _LOCAL else "Shared"
    cx.xn1g = nc.dram_tensor("xn1g", [D, S], BF16, kind="Internal", addr_space=_spc)
    cx.xn1p = nc.dram_tensor("xn1p", [D, 2 * QTOK], BF16, kind="Internal", addr_space=_spc)
    cx.attnx = nc.dram_tensor("attnx", [D, 2 * QTOK], BF16, kind="Internal", addr_space=_spc)
    cx.flg = nc.dram_tensor("flg", [2, 16], I32, kind="Internal", addr_space=_spc)
    cx.x1park = nc.dram_tensor("x1park", [D, QTOK], F32, kind="Internal")

    for s in ("LD", "LDX", "LDA", "LDB", "LDAL", "WGT", "WGTA", "WGTB", "WGTC",
              "WGTDA", "WGTDB", "PE", "DV", "SC", "ST", "BAR", "GPD", "PF"):
        plan.sem(s)

    for nm_d, nm_s in [("bqkv_sb", "bqkv"), ("bout_sb", "bout"), ("gb_sb", "gb"),
                       ("upb_sb", "upb"), ("gpb_sb", "gpb"),
                       ("mask_sb", "maskc"), ("iota_sb", "iota8"),
                       ("ident_sb", "ident"), ("onesc_sb", "onesc"),
                       ("onesl_sb", "onesl"), ("sel8_sb", "sel8"),
                       ("offs_sb", "offs"), ("flagv_sb", "flagv")]:
        cx.I("sync", lambda e, c, nm_d=nm_d, nm_s=nm_s: e.dma_start(
            out=c.st[nm_d][:, :], in_=c.dr[nm_s].ap()), "LD", 16)
    cx.I("gpsimd", lambda e, c: e.dma_start(out=c.st["gw_sb"][:, :].rearrange("p (dc e) -> p dc e", dc=DC), in_=c.dr["gwT"].ap().rearrange("(dc p) e -> p dc e", p=128)), "WGT", 16)
    cx.I("gpsimd", lambda e, c: e.dma_start(out=c.st["wq_sb"][:, :].rearrange("p (dc h) -> p dc h", dc=DC), in_=c.dr["wq"].ap().rearrange("(dc p) h -> p dc h", p=128)), "WGT", 16)
    cx.I("gpsimd", lambda e, c: e.dma_start(out=c.st["wk_sb"][:, :].rearrange("p (dc h) -> p dc h", dc=DC), in_=c.dr["wk"].ap().rearrange("(dc p) h -> p dc h", p=128)), "WGT", 16)
    cx.I("gpsimd", lambda e, c: e.dma_start(out=c.st["wv_sb"][:, :].rearrange("p (dc h) -> p dc h", dc=DC), in_=c.dr["wv"].ap().rearrange("(dc p) h -> p dc h", p=128)), "WGT", 16)
    cx.I("gpsimd", lambda e, c: e.dma_start(out=c.st["wo_sb"][:, :].rearrange("p (dc d) -> p dc d", dc=DC), in_=c.dr["wout"].ap().rearrange("(dc p) d -> p dc d", p=128)), "WGT", 16)
    cx.I("gpsimd", lambda e, c: e.dma_start(out=c.st["dbn_sb"][0:8, :], in_=c.dr["dbn"].ap()), "WGT", 16)

    ld_all = plan.val("LD")
    wgt_all = plan.val("WGT")
    for engn in ENGINES:
        cx.W(engn, "LD", ld_all)
        cx.W(engn, "WGT", wgt_all)

    cx.CH("vector", lambda e, c: e.tensor_scalar(
        c.st["bqkv8_sb"][:, :], c.st["bqkv_sb"][:, :], 0.125, None, OP.mult), "DV")

    def f_regs_sync(eng, c):
        r = c.regs["sync"]
        eng.reg_load(r["rP"], c.st["offs_sb"][0:1, 1:2])
        eng.reg_load(r["rA"], c.st["offs_sb"][0:1, 2:3])
        eng.reg_load(r["rO"], c.st["offs_sb"][0:1, 3:4])
    cx.N("sync", f_regs_sync)

    def f_regs_gp(eng, c):
        r = c.regs["gpsimd"]
        eng.reg_load(r["rFmB0"], c.st["offs_sb"][0:1, 4:5])
        eng.reg_load(r["rFpB0"], c.st["offs_sb"][0:1, 5:6])
        eng.reg_alu(r["rFmB1"], r["rFmB0"], 4, OP.add)
        eng.reg_alu(r["rFpB1"], r["rFpB0"], 4, OP.add)
        eng.reg_mov(r["rW"], 0)
    cx.N("gpsimd", f_regs_gp)

    if _STAGE == 0:
        def f_lx0(eng, c):
            src_ = bass.AP(c.dr["xin"], 0, [[TOKS_CORE, 128], [128 * TOKS_CORE, DC], [1, TC]])
            return eng.dma_start(out=c.st["x_ch"][:, :, :], in_=src_)
        cx.I("sync", f_lx0, "LDX", 16)
        cx.W("sync", "LDX", plan.val("LDX"))
        def f_o0(eng, c):
            dst = c.dr["out"].ap()[:, 0:TC].rearrange("(dc p) t -> p dc t", p=128)
            return eng.dma_start(out=dst, in_=c.st["x_ch"][:, :, :])
        cx.I("sync", f_o0, "ST", 16)
        cx.W("sync", "ST", plan.val("ST"))
    if _STAGE >= 1:
        emit_ffn1(cx)
    if _STAGE >= 2:
        emit_barrier(cx, 1)
        bar1 = plan.val("BAR")
        for engn in ("tensor", "vector", "scalar", "sync"):
            cx.W(engn, "BAR", bar1)
        if _KATT != "none":
            emit_attention(cx)
    if _STAGE >= 3:
        emit_barrier(cx, 2)
        bar2 = plan.val("BAR")
        for engn in ("tensor", "vector", "scalar", "sync"):
            cx.W(engn, "BAR", bar2)
        emit_outproj_ffn2(cx)
    if _STAGE == 1:
        cx.W("sync", "ST", plan.val("ST"))
        def f_dbg(eng, c):
            src_ = bass.AP(c.x1park, 0, [[QTOK, 128], [128 * QTOK, DC], [1, QTOK]])
            dst = c.dr["out"].ap().rearrange("(dc p) t -> p dc t", p=128)
            return eng.dma_start(out=dst, in_=src_)
        cx.I("sync", f_dbg, "ST", 16)
    if _STAGE == 2:
        cx.W("gpsimd", "BAR", plan.val("BAR"))
        cx.W("gpsimd", "ST", plan.val("ST"))
        def f_dbg2(eng, c):
            r = c.regs["gpsimd"]
            eng.reg_load(r["rEx"], c.st["offs_sb"][0:1, 3:4])
            src_ = bass.AP(c.attnx, r["rEx"], [[4096, 128], [128 * 4096, DC], [1, QTOK]])
            dst = c.dr["out"].ap().rearrange("(dc p) t -> p dc t", p=128)
            return eng.dma_start(out=dst, in_=src_)
        cx.I("gpsimd", f_dbg2, "GPD", 16)
        cx.W("gpsimd", "GPD", plan.val("GPD"))
    cx.W("sync", "ST", plan.val("ST"))

    sbufs = [
        ("x_ch", [128, DC, TC], F32),
        ("xn_ch", [128, DC, TC], BF16),
        ("xsq_s", [128, TC], BF16),
        ("xb_s", [128, TC], BF16),
        ("arena", [128, 26 * 1024], BF16),
        ("wu_a", [128, DC * 256], BF16), ("wu_b", [128, DC * 256], BF16),
        ("wu_c", [128, DC * 256], BF16),
        ("wd_a", [128, D], BF16), ("wd_b", [128, D], BF16),
        ("wq_sb", [128, DC * 512], BF16), ("wk_sb", [128, DC * 512], BF16),
        ("wv_sb", [128, DC * 512], BF16),
        ("wo_sb", [128, DC * 1024], BF16),
        ("bqkv_sb", [128, 12], F32),
        ("bqkv8_sb", [128, 12], F32),
        ("bout_sb", [128, DC], F32),
        ("gw_sb", [128, DC * 8], BF16),
        ("gb_sb", [E, 1], F32),
        ("upb_sb", [128, E * FC], F32), ("gpb_sb", [128, E * FC], F32),
        ("dbn_sb", [E, D], BF16),
        ("mask_sb", [128, 4 * 512], BF16),
        ("iota_sb", [128, E], F32),
        ("ident_sb", [128, 128], F32),
        ("onesc_sb", [128, 1], BF16),
        ("onesl_sb", [1, 128], BF16),
        ("sel8_sb", [E, E * 128], BF16),
        ("offs_sb", [1, 64], I32),
        ("flagv_sb", [2, 16], I32),
        ("flgrd_sb", [1, 16], I32),
        ("gsc", [E, TC], F32),
        ("gtm", [128, 8 * E], F32),
        ("gm1", [128, 8], F32), ("gm2", [128, 8], F32),
        ("gi", [128, 8], F32),
        ("gtmp", [128, E], F32),
        ("gmask1", [128, 8 * E], F32), ("gmask2", [128, 8 * E], F32),
        ("gd21", [128, 8], F32),
        ("gw1", [128, 8], F32), ("gw2", [128, 8], F32),
        ("gcomb", [128, 8 * E], F32),
        ("combh", [E, TC], BF16),
        ("wbc", [128, TC], BF16),
        ("lnA", [1, 1024], F32), ("lnB", [1, 1024], F32), ("lnC", [1, 1024], F32),
        ("lnb", [1, 1024], BF16),
        ("lnt", [128, 512], F32),
        ("upev", [128, TC], F32),
        ("gpev", [128, TC], F32),
        ("hsil", [128, TC], BF16),
    ]
    psums = [("p%d" % i, [128, 512], F32) for i in range(8)]

    with ExitStack() as es:
        for nm, shape, dt in sbufs:
            cx.st[nm] = es.enter_context(nc.sbuf_tensor(nm, shape, dt))
        cx.st["x2t"] = cx.st["lnt"][:, :]
        ar = cx.st["arena"]
        cx.st["hprime"] = ar[:, :].rearrange("p (f t) -> p f t", f=FC)
        cx.st["k_sb"] = ar[:, 0:8192]
        cx.st["v_sb"] = ar[:, 8192:16384]
        cx.st["q_sb"] = ar[:, 16384:20480]
        cx.st["xnt_a"] = ar[:, 20480:22528]
        cx.st["xnt_b"] = ar[:, 22528:24576]
        cx.st["pt_a"] = ar[:, 24576:25088]
        cx.st["pt_b"] = ar[:, 25088:25600]
        cx.st["aall"] = ar[:, 0:4096].rearrange("p (dc t) -> p dc t", dc=DC)
        cx.st["att_hp"] = ar[:, 20480:24576]
        cx.st["rden"] = cx.st["lnA"][0:1, 0:512]
        cx.st["rdenb"] = cx.st["lnb"][0:1, 0:512]
        cx.st["rbc"] = cx.st["lnt"][:, 0:256]
        cx.st["avn"] = cx.st["lnt"][:, 256:512]
        for nm, shape, dt in psums:
            cx.st[nm] = es.enter_context(nc.psum_tensor(nm, shape, dt))
        for sname in plan.sem_names:
            cx.sems[sname] = es.enter_context(nc.semaphore(sname))
        block = es.enter_context(nc.Block())

        @block.sync
        def _(eng):
            with ExitStack() as res:
                cx.regs["sync"] = {k: res.enter_context(eng.register(k))
                                   for k in ("rP", "rA", "rO", "rT", "rU")}
                for fn in plan.ops["sync"]:
                    fn(eng, cx)

        @block.gpsimd
        def _(eng):
            with ExitStack() as res:
                cx.regs["gpsimd"] = {k: res.enter_context(eng.register(k))
                                     for k in ("rFmB0", "rFpB0", "rFmB1", "rFpB1",
                                               "rEx", "rV", "rC", "rW")}
                for fn in plan.ops["gpsimd"]:
                    fn(eng, cx)

        @block.tensor
        def _(eng):
            for fn in plan.ops["tensor"]:
                fn(eng, cx)

        @block.vector
        def _(eng):
            for fn in plan.ops["vector"]:
                fn(eng, cx)

        @block.scalar
        def _(eng):
            for fn in plan.ops["scalar"]:
                fn(eng, cx)

    return nc



def _stage_core(core, inp, nonce, shared):
    g = core // 2
    batch = g // 2
    a = g % 2
    p = core % 2

    x = shared["x"]
    m = {}
    tok_idx = []
    for i in range(32):
        if i < 16:
            gt = 2 * (p * 16 + i) + a
        else:
            gt = 2 * (p * 16 + (i - 16)) + (1 - a)
        tok_idx.extend(range(gt * 128, (gt + 1) * 128))
    m["xin"] = np.ascontiguousarray(x[batch][tok_idx, :].T)
    m["wq"] = shared["wq"][p]
    m["wk"] = shared["wk"][p]
    m["wv"] = shared["wv"][p]
    m["bqkv"] = shared["bqkv"][p]
    m["wout"] = shared["wout"]
    m["bout"] = shared["bout"]
    m["gwT"] = shared["gwT"]
    m["gb"] = shared["gb"]
    m["upgp"] = shared["upgp"]
    m["dwn"] = shared["dwn"]
    m["upb"] = shared["upb"]
    m["gpb"] = shared["gpb"]
    m["dbn"] = shared["dbn"]
    m["maskc"] = shared["maskc"][a]
    m["iota8"] = shared["iota8"]
    m["ident"] = shared["ident"]
    m["onesc"] = shared["onesc"]
    m["onesl"] = shared["onesl"]
    m["sel8"] = shared["sel8"]

    offs = np.zeros((1, 64), np.int32)
    offs[0, 1] = p * 2048
    offs[0, 2] = p * 512 * 4096
    offs[0, 3] = p * 2048
    offs[0, 4] = p * 16
    offs[0, 5] = (1 - p) * 16
    for ch in range(4):
        if ch < 2:
            side = a
            tb = p * 16 + ch * 8
        else:
            side = 1 - a
            tb = p * 16 + (ch - 2) * 8
        offs[0, 6 + ch] = side * 4096 + tb * 128
    m["offs"] = offs
    flagv = np.zeros((2, 16), np.int32)
    flagv[0, :] = nonce + 1
    flagv[1, :] = nonce + 2
    m["flagv"] = flagv
    return m


def _stage_shared(inp):
    sh = {}
    sh["x"] = np.asarray(inp["L0_attn_x"], np.float32)
    W_qkv = np.asarray(inp["L0_attn_W_qkv"], np.float32)
    b_qkv = np.asarray(inp["L0_attn_b_qkv"], np.float32)
    sh["wout"] = np.ascontiguousarray(np.asarray(inp["L0_attn_W_out"], np.float32))
    b_out = np.asarray(inp["L0_attn_b_out"], np.float32)
    sh["bout"] = np.ascontiguousarray(b_out.reshape(DC, 128).T)
    gate_w = np.asarray(inp["L0_ffn_gate_w"], np.float32)
    sh["gwT"] = np.ascontiguousarray(gate_w.T)
    sh["gb"] = np.ascontiguousarray(np.asarray(inp["L0_ffn_gate_b"], np.float32).reshape(E, 1))
    up_w = np.asarray(inp["L0_ffn_up_w"], np.float32)
    up_b = np.asarray(inp["L0_ffn_up_b"], np.float32)
    gp_w = np.asarray(inp["L0_ffn_gp_w"], np.float32)
    gp_b = np.asarray(inp["L0_ffn_gp_b"], np.float32)
    down_w = np.asarray(inp["L0_ffn_down_w"], np.float32)
    down_b = np.asarray(inp["L0_ffn_down_b"], np.float32)

    sh["wq"], sh["wk"], sh["wv"], sh["bqkv"] = [], [], [], []
    for p in range(2):
        heads = list(range(p * 8, p * 8 + 8))
        qcols, kcols, vcols = [], [], []
        for h in heads:
            qcols += list(range(h * 192, h * 192 + 64))
            kcols += list(range(h * 192 + 64, h * 192 + 128))
            vcols += list(range(h * 192 + 128, h * 192 + 192))
        sh["wq"].append(np.ascontiguousarray(W_qkv[:, qcols]))
        sh["wk"].append(np.ascontiguousarray(W_qkv[:, kcols]))
        sh["wv"].append(np.ascontiguousarray(W_qkv[:, vcols]))
        bq = b_qkv[qcols].reshape(4, 2, 64)
        bk = b_qkv[kcols].reshape(4, 2, 64)
        bv = b_qkv[vcols].reshape(4, 2, 64)
        bqkv = np.zeros((128, 12), np.float32)
        for hp in range(4):
            bqkv[:, hp * 3 + 0] = bq[hp].reshape(128)
            bqkv[:, hp * 3 + 1] = bk[hp].reshape(128)
            bqkv[:, hp * 3 + 2] = bv[hp].reshape(128)
        sh["bqkv"].append(bqkv)

    upgp = np.zeros((E, FC, D, 256), np.float32)
    upT = np.transpose(up_w, (0, 2, 1))
    gpT = np.transpose(gp_w, (0, 2, 1))
    for fch in range(FC):
        f0 = fch * 128
        f1 = min(f0 + 128, F)
        w = f1 - f0
        upgp[:, fch, :, 0:w] = upT[:, :, f0:f1]
        upgp[:, fch, :, 128:128 + w] = gpT[:, :, f0:f1]
    sh["upgp"] = np.ascontiguousarray(
        upgp.reshape(E, FC, DC, 128, 256).transpose(0, 1, 3, 2, 4).reshape(E, FC, 128, DC * 256))
    dwn = np.zeros((E, FC, 128, D), np.float32)
    dwnT = np.transpose(down_w, (0, 2, 1))
    for fch in range(FC):
        f0 = fch * 128
        f1 = min(f0 + 128, F)
        dwn[:, fch, 0:f1 - f0, :] = dwnT[:, f0:f1, :]
    sh["dwn"] = dwn

    def bias_grid(b):
        g_ = np.zeros((128, E * FC), np.float32)
        for e_i in range(E):
            for fch in range(FC):
                f0 = fch * 128
                f1 = min(f0 + 128, F)
                g_[0:f1 - f0, e_i * FC + fch] = b[e_i, f0:f1]
        return g_
    sh["upb"] = bias_grid(up_b)
    sh["gpb"] = bias_grid(gp_b)
    sh["dbn"] = np.ascontiguousarray(down_b)

    sh["maskc"] = []
    kk = np.arange(128)[:, None]
    qq = np.arange(128)[None, :]
    for a in range(2):
        maskc = np.zeros((128, 4 * 512), np.float32)
        for r in range(4):
            mlo = (r * 128 + kk <= a * 128 + qq).astype(np.float32)
            mhi = (r * 128 + kk <= (2 + a) * 128 + qq).astype(np.float32)
            blk = np.concatenate([mlo, mhi], axis=1)
            maskc[:, r * 512:r * 512 + 256] = blk
            maskc[:, r * 512 + 256:(r + 1) * 512] = blk
        sh["maskc"].append(maskc.astype(bfdt))

    sh["iota8"] = np.broadcast_to(np.arange(E, dtype=np.float32), (128, E)).copy()
    sh["ident"] = np.eye(128, dtype=np.float32)
    sh["onesc"] = np.ones((128, 1), bfdt)
    sh["onesl"] = np.ones((1, 128), bfdt)
    sel8 = np.zeros((E, E * 128), np.float32)
    for e_i in range(E):
        sel8[e_i, e_i * 128:(e_i + 1) * 128] = 1.0
    sh["sel8"] = sel8.astype(bfdt)
    return sh


def kernel(**inputs):
    nonce = int(np.random.default_rng().integers(1, 1 << 24)) * 4
    nc = build_nc()
    shared = _stage_shared(inputs)
    in_maps = [_stage_core(c, inputs, nonce, shared) for c in range(8)]
    res = run_bass_kernel_spmd(nc, in_maps, core_ids=list(range(8)))

    out = np.zeros((2, S, D), np.float32)
    for core in range(8):
        g = core // 2
        batch = g // 2
        a = g % 2
        p = core % 2
        y = res.results[core]["out"]
        for jj in range(16):
            j = p * 16 + jj
            gt = 2 * j + a
            out[batch, gt * 128:(gt + 1) * 128, :] = y[:, jj * 128:(jj + 1) * 128].T
    return out


# version 13 (final)
# speedup vs baseline: 1.2105x; 1.2105x over previous
import os
import numpy as np
import ml_dtypes
from contextlib import ExitStack

_STAGE = int(os.environ.get("KSTAGE", "3"))
_KATT = os.environ.get("KATT", "full")
_NORACE = os.environ.get("KNORACE", "1") == "1"
_LOCAL = os.environ.get("KLOCAL", "0") == "1"

import concourse.bass as bass
import concourse.mybir as mybir
from concourse.bass_utils import run_bass_kernel_spmd

F32 = mybir.dt.float32
BF16 = mybir.dt.bfloat16
I32 = mybir.dt.int32
AF = mybir.ActivationFunctionType
OP = mybir.AluOpType
AXX = mybir.AxisListType.X

D = 1024
DC = 8
S = 8192
NHEAD = 16
HD = 64
E = 8
F = 3264
FC = 26
TOKS_CORE = 4096
QTOK = 2048
TC = 1024
NCH1 = 4
NCH2 = 2
BIG = 1.0e9
EPS = 1e-5
bfdt = ml_dtypes.bfloat16

ENGINES = ("sync", "gpsimd", "tensor", "vector", "scalar")


class Plan:
    def __init__(self):
        self.ops = {e: [] for e in ENGINES}
        self.sem_names = []
        self.counts = {}

    def sem(self, name):
        if name not in self.counts:
            self.sem_names.append(name)
            self.counts[name] = 0
        return name

    def inc(self, name, by):
        self.counts[name] += by
        return self.counts[name]

    def val(self, name):
        return self.counts[name]


class Ctx:
    def __init__(self, nc, plan):
        self.nc = nc
        self.plan = plan
        self.dr = {}
        self.st = {}
        self.regs = {}
        self.sems = {}
        self.peA_prev = None
        self.peB_prev = None
        self.wu_last_pe = {}
        self.wd_last_pe = {}

    def emit(self, engine, fn):
        self.plan.ops[engine].append(fn)

    def W(self, engine, sem, upto=None):
        v = self.plan.val(sem) if upto is None else upto
        def f(eng, cx):
            eng.wait_ge(cx.sems[sem], v)
        self.emit(engine, f)

    def I(self, engine, fn, sem, by):
        self.plan.inc(sem, by)
        def f(eng, cx):
            fn(eng, cx).then_inc(cx.sems[sem], by)
        self.emit(engine, f)

    def N(self, engine, fn):
        self.emit(engine, lambda eng, cx, fn=fn: fn(eng, cx))

    def CH(self, engine, fn, sem):
        v = self.plan.inc(sem, 1)
        def f(eng, cx):
            fn(eng, cx).then_inc(cx.sems[sem], 1)
            eng.wait_ge(cx.sems[sem], v)
        self.emit(engine, f)



def sub_ap(view, extra_off, pattern):
    if isinstance(view, bass.AP):
        return bass.AP(view.tensor, view.offset + extra_off, pattern)
    return bass.AP(view, extra_off, pattern)


def emit_ln(cx, src_name, dst_name):
    for dc in range(DC):
        cx.W("vector", "PE")
        cx.CH("vector", lambda e, c, dc=dc: e.tensor_copy(c.st["xb_s"][:, :], c.st[src_name][:, dc, :]), "DV")
        cx.W("scalar", "PE")
        cx.W("scalar", "DV")
        cx.I("scalar", lambda e, c, dc=dc: e.activation(c.st["xsq_s"][:, :], c.st[src_name][:, dc, :], AF.Square), "SC", 1)
        cx.W("tensor", "DV")
        cx.W("tensor", "SC")
        for th in range(2):
            sl = slice(th * 512, (th + 1) * 512)
            cx.N("tensor", lambda e, c, dc=dc, sl=sl, th=th: e.matmul(
                c.st["p%d" % th][0:1, :], c.st["onesc_sb"][:, 0:1], c.st["xb_s"][:, sl],
                start=(dc == 0), stop=(dc == DC - 1)))
            emitter = cx.N if th == 0 else (lambda eng, fn: cx.I(eng, fn, "PE", 1))
            emitter("tensor", lambda e, c, dc=dc, sl=sl, th=th: e.matmul(
                c.st["p%d" % (2 + th)][0:1, :], c.st["onesc_sb"][:, 0:1], c.st["xsq_s"][:, sl],
                start=(dc == 0), stop=(dc == DC - 1)))
    for th in range(2):
        sl = slice(th * 512, (th + 1) * 512)
        cx.W("vector", "PE")
        cx.CH("vector", lambda e, c, th=th: e.tensor_scalar(c.st["lnA"][0:1, 0:512], c.st["p%d" % th][0:1, :], 1.0 / D, None, OP.mult), "DV")
        cx.CH("vector", lambda e, c, th=th: e.tensor_scalar(c.st["lnA"][0:1, 512:1024], c.st["p%d" % (2 + th)][0:1, :], 1.0 / D, None, OP.mult), "DV")
        cx.CH("vector", lambda e, c: e.tensor_mul(c.st["lnB"][0:1, 0:512], c.st["lnA"][0:1, 0:512], c.st["lnA"][0:1, 0:512]), "DV")
        cx.CH("vector", lambda e, c: e.tensor_sub(c.st["lnB"][0:1, 0:512], c.st["lnA"][0:1, 512:1024], c.st["lnB"][0:1, 0:512]), "DV")
        cx.CH("vector", lambda e, c: e.tensor_scalar(c.st["lnB"][0:1, 0:512], c.st["lnB"][0:1, 0:512], EPS, None, OP.add), "DV")
        cx.W("scalar", "DV")
        cx.I("scalar", lambda e, c: e.activation(c.st["lnB"][0:1, 512:1024], c.st["lnB"][0:1, 0:512], AF.Sqrt), "SC", 1)
        cx.W("vector", "SC")
        cx.CH("vector", lambda e, c: e.reciprocal(c.st["lnC"][0:1, 0:512], c.st["lnB"][0:1, 512:1024]), "DV")
        cx.CH("vector", lambda e, c: e.tensor_mul(c.st["lnC"][0:1, 512:1024], c.st["lnA"][0:1, 0:512], c.st["lnC"][0:1, 0:512]), "DV")
        cx.CH("vector", lambda e, c: e.tensor_copy(c.st["lnb"][0:1, :], c.st["lnC"][0:1, :]), "DV")
        cx.W("tensor", "DV")
        pr = "p%d" % th
        pn = "p%d" % (2 + th)
        cx.N("tensor", lambda e, c, pr=pr: e.matmul(c.st[pr][:, :], c.st["onesl_sb"][0:1, :], c.st["lnb"][0:1, 0:512], start=True, stop=True))
        cx.I("tensor", lambda e, c, pn=pn: e.matmul(c.st[pn][:, :], c.st["onesl_sb"][0:1, :], c.st["lnb"][0:1, 512:1024], start=True, stop=True), "PE", 1)
        cx.W("vector", "PE")
        for dc in range(DC):
            cx.CH("vector", lambda e, c, dc=dc, sl=sl, pn=pn: e.tensor_tensor(
                c.st["lnt"][:, :], c.st[src_name][:, dc, sl], c.st[pn][:, :], OP.subtract), "DV")
            cx.CH("vector", lambda e, c, dc=dc, sl=sl, pr=pr: e.tensor_tensor(
                c.st[dst_name][:, dc, sl], c.st["lnt"][:, :], c.st[pr][:, :], OP.mult), "DV")


def emit_gate(cx):
    for th in range(2):
        sl = slice(th * 512, (th + 1) * 512)
        ps = "p4" if th == 0 else "p5"
        cx.W("tensor", "DV")
        for dc in range(DC):
            emitter = cx.N if dc < DC - 1 else (lambda eng, fn: cx.I(eng, fn, "PE", 1))
            emitter("tensor", lambda e, c, dc=dc, sl=sl, ps=ps: e.matmul(
                c.st[ps][0:8, :], c.st["gw_sb"][:, dc * 8:(dc + 1) * 8], c.st["xn_ch"][:, dc, sl],
                start=(dc == 0), stop=(dc == DC - 1)))
        cx.W("scalar", "PE")
        cx.I("scalar", lambda e, c, th=th, ps=ps: e.activation(
            c.st["gsc"][0:8, th * 512:(th + 1) * 512], c.st[ps][0:8, :], AF.Identity,
            bias=c.st["gb_sb"][0:8, 0:1]), "SC", 1)
    cx.W("tensor", "SC")
    for i in range(8):
        emitter = cx.N if i < 7 else (lambda eng, fn: cx.I(eng, fn, "PE", 1))
        emitter("tensor", lambda e, c, i=i: e.transpose(
            c.st["p6"][:, i * 8:(i + 1) * 8], c.st["gsc"][0:8, i * 128:(i + 1) * 128], c.st["ident_sb"][0:8, 0:8]))
    cx.W("vector", "PE")
    cx.CH("vector", lambda e, c: e.tensor_copy(c.st["gtm"][:, :], c.st["p6"][:, 0:64]), "DV")
    for i in range(8):
        tsl = slice(i * 8, (i + 1) * 8)
        cx.CH("vector", lambda e, c, tsl=tsl, i=i: e.reduce_max(c.st["gm1"][:, i:i + 1], c.st["gtm"][:, tsl], axis=AXX), "DV")
        cx.CH("vector", lambda e, c, tsl=tsl, i=i: e.tensor_scalar(c.st["gtmp"][:, :], c.st["gtm"][:, tsl], c.st["gm1"][:, i:i + 1], None, OP.is_equal), "DV")
        cx.CH("vector", lambda e, c: e.tensor_scalar(c.st["gtmp"][:, :], c.st["gtmp"][:, :], -BIG, BIG, OP.mult, OP.add), "DV")
        cx.CH("vector", lambda e, c: e.tensor_tensor(c.st["gtmp"][:, :], c.st["gtmp"][:, :], c.st["iota_sb"][:, :], OP.add), "DV")
        cx.CH("vector", lambda e, c, i=i: e.tensor_reduce(c.st["gi"][:, i:i + 1], c.st["gtmp"][:, :], axis=AXX, op=OP.min), "DV")
        cx.CH("vector", lambda e, c, tsl=tsl, i=i: e.tensor_scalar(c.st["gmask1"][:, tsl], c.st["iota_sb"][:, :], c.st["gi"][:, i:i + 1], None, OP.is_equal), "DV")
        cx.CH("vector", lambda e, c, tsl=tsl: e.scalar_tensor_tensor(c.st["gtmp"][:, :], c.st["gmask1"][:, tsl], -BIG, c.st["gtm"][:, tsl], OP.mult, OP.add), "DV")
        cx.CH("vector", lambda e, c, i=i: e.reduce_max(c.st["gm2"][:, i:i + 1], c.st["gtmp"][:, :], axis=AXX), "DV")
        cx.CH("vector", lambda e, c, i=i: e.tensor_scalar(c.st["gtmp"][:, :], c.st["gtmp"][:, :], c.st["gm2"][:, i:i + 1], None, OP.is_equal), "DV")
        cx.CH("vector", lambda e, c: e.tensor_scalar(c.st["gtmp"][:, :], c.st["gtmp"][:, :], -BIG, BIG, OP.mult, OP.add), "DV")
        cx.CH("vector", lambda e, c: e.tensor_tensor(c.st["gtmp"][:, :], c.st["gtmp"][:, :], c.st["iota_sb"][:, :], OP.add), "DV")
        cx.CH("vector", lambda e, c, i=i: e.tensor_reduce(c.st["gi"][:, i:i + 1], c.st["gtmp"][:, :], axis=AXX, op=OP.min), "DV")
        cx.CH("vector", lambda e, c, tsl=tsl, i=i: e.tensor_scalar(c.st["gmask2"][:, tsl], c.st["iota_sb"][:, :], c.st["gi"][:, i:i + 1], None, OP.is_equal), "DV")
        cx.CH("vector", lambda e, c, i=i: e.tensor_sub(c.st["gd21"][:, i:i + 1], c.st["gm2"][:, i:i + 1], c.st["gm1"][:, i:i + 1]), "DV")
    cx.W("scalar", "DV")
    cx.I("scalar", lambda e, c: e.activation(c.st["gw2"][:, :], c.st["gd21"][:, :], AF.Sigmoid), "SC", 1)
    cx.W("vector", "SC")
    cx.CH("vector", lambda e, c: e.tensor_scalar(c.st["gw1"][:, :], c.st["gw2"][:, :], -1.0, 1.0, OP.mult, OP.add), "DV")
    for i in range(8):
        tsl = slice(i * 8, (i + 1) * 8)
        cx.CH("vector", lambda e, c, tsl=tsl, i=i: e.tensor_scalar(c.st["gmask1"][:, tsl], c.st["gmask1"][:, tsl], c.st["gw1"][:, i:i + 1], None, OP.mult), "DV")
        cx.CH("vector", lambda e, c, tsl=tsl, i=i: e.tensor_scalar(c.st["gmask2"][:, tsl], c.st["gmask2"][:, tsl], c.st["gw2"][:, i:i + 1], None, OP.mult), "DV")
        cx.CH("vector", lambda e, c, tsl=tsl: e.tensor_add(c.st["gcomb"][:, tsl], c.st["gmask1"][:, tsl], c.st["gmask2"][:, tsl]), "DV")
    cx.W("tensor", "DV")
    for i in range(8):
        ps = "p6" if i < 4 else "p7"
        off = (i % 4) * 128
        emitter = cx.N if i < 7 else (lambda eng, fn: cx.I(eng, fn, "PE", 1))
        emitter("tensor", lambda e, c, i=i, ps=ps, off=off: e.transpose(
            c.st[ps][0:8, off:off + 128], c.st["gcomb"][:, i * 8:(i + 1) * 8], c.st["ident_sb"][:, :]))
    cx.W("vector", "PE")
    cx.CH("vector", lambda e, c: e.tensor_scalar(c.st["combh"][0:8, 0:512], c.st["p6"][0:8, :], 0.5, None, OP.mult), "DV")
    cx.CH("vector", lambda e, c: e.tensor_scalar(c.st["combh"][0:8, 512:1024], c.st["p7"][0:8, :], 0.5, None, OP.mult), "DV")


def emit_moe(cx):
    plan = cx.plan
    for e_i in range(E):
        cx.W("tensor", "DV")
        cx.W("tensor", "SC")
        cx.N("tensor", lambda e, c, e_i=e_i: e.matmul(
            c.st["p4"][:, :], c.st["sel8_sb"][0:8, e_i * 128:(e_i + 1) * 128],
            c.st["combh"][0:8, 0:512], start=True, stop=True))
        cx.I("tensor", lambda e, c, e_i=e_i: e.matmul(
            c.st["p5"][:, :], c.st["sel8_sb"][0:8, e_i * 128:(e_i + 1) * 128],
            c.st["combh"][0:8, 512:1024], start=True, stop=True), "PE", 1)
        cx.W("vector", "PE")
        cx.CH("vector", lambda e, c: e.tensor_copy(c.st["wbc"][:, 0:512], c.st["p4"][:, :]), "DV")
        cx.CH("vector", lambda e, c: e.tensor_copy(c.st["wbc"][:, 512:1024], c.st["p5"][:, :]), "DV")

        pe_after = {}
        dv_after = {}
        for f in range(FC):
            bi = f % 3
            buf = ("wu_a", "wu_b", "wu_c")[bi]
            pus = ("p0", "p2") if f % 2 == 0 else ("p4", "p6")
            pgs = ("p1", "p3") if f % 2 == 0 else ("p5", "p7")
            lp = cx.wu_last_pe.get(bi)
            if lp is not None:
                cx.W("gpsimd", "PE", lp)
            else:
                cx.W("gpsimd", "PE")
            wsem = ("WGTA", "WGTB", "WGTC")[bi]
            cx.I("gpsimd", lambda e, c, e_i=e_i, f=f, buf=buf: e.dma_start(
                out=c.st[buf][:, :], in_=c.dr["upgp"].ap()[e_i, f]), wsem, 16)
            cx.W("tensor", wsem)
            if f >= 2:
                cx.W("tensor", "DV", dv_after[f - 2])
            else:
                cx.W("tensor", "DV")
            for th in range(2):
                for dc in range(DC):
                    cx.N("tensor", lambda e, c, dc=dc, th=th, buf=buf, pus=pus: e.matmul(
                        c.st[pus[th]][:, :], c.st[buf][:, dc * 256:dc * 256 + 128],
                        c.st["xn_ch"][:, dc, th * 512:(th + 1) * 512],
                        start=(dc == 0), stop=(dc == DC - 1)))
                for dc in range(DC):
                    emitter = cx.N if not (th == 1 and dc == DC - 1) else (lambda eng, fn: cx.I(eng, fn, "PE", 1))
                    emitter("tensor", lambda e, c, dc=dc, th=th, buf=buf, pgs=pgs: e.matmul(
                        c.st[pgs[th]][:, :], c.st[buf][:, dc * 256 + 128:dc * 256 + 256],
                        c.st["xn_ch"][:, dc, th * 512:(th + 1) * 512],
                        start=(dc == 0), stop=(dc == DC - 1)))
            pe_after[f] = plan.val("PE")
            cx.wu_last_pe[bi] = pe_after[f]
            cx.W("scalar", "PE")
            cx.W("scalar", "DV")
            cx.W("vector", "PE")
            for th in range(2):
                tsl = slice(th * 512, (th + 1) * 512)
                bcol = e_i * FC + f
                cx.I("scalar", lambda e, c, pus=pus, th=th, tsl=tsl, bcol=bcol: e.activation(
                    c.st["upev"][:, tsl], c.st[pus[th]][:, :], AF.Identity,
                    bias=c.st["upb_sb"][:, bcol:bcol + 1]), "SC", 1)
                cx.CH("vector", lambda e, c, pgs=pgs, th=th, tsl=tsl, bcol=bcol: e.tensor_scalar(
                    c.st["gpev"][:, tsl], c.st[pgs[th]][:, :], c.st["gpb_sb"][:, bcol:bcol + 1], None, OP.add), "DV")
                cx.W("vector", "SC")
                cx.CH("vector", lambda e, c, tsl=tsl: e.tensor_tensor(
                    c.st["upev"][:, tsl], c.st["upev"][:, tsl], c.st["gpev"][:, tsl], OP.mult), "DV")
                cx.W("scalar", "DV")
                cx.I("scalar", lambda e, c, tsl=tsl: e.activation(
                    c.st["hsil"][:, tsl], c.st["upev"][:, tsl], AF.Silu), "SC", 1)
                cx.W("vector", "SC")
                cx.I("vector", lambda e, c, f=f, tsl=tsl: e.tensor_tensor(
                    c.st["hprime"][:, f, tsl], c.st["hsil"][:, tsl], c.st["wbc"][:, tsl], OP.mult), "DV", 1)
            dv_after[f] = plan.val("DV")
        cx.peA_prev = pe_after[FC - 1]

        for th in range(2):
            tsl = slice(th * 512, (th + 1) * 512)
            pe_b = {}
            for f in range(FC):
                bi = f % 4
                buf = ("wd_a", "wd_b", "wd_c", "wd_d")[bi]
                lp = cx.wd_last_pe.get(bi)
                if lp is not None:
                    cx.W("gpsimd", "PE", lp)
                else:
                    cx.W("gpsimd", "PE")
                wsem = ("WGTDA", "WGTDB", "WGTDC", "WGTDD")[bi]
                cx.I("gpsimd", lambda e, c, e_i=e_i, f=f, buf=buf: e.dma_start(
                    out=c.st[buf][:, :], in_=c.dr["dwn"].ap()[e_i, f]), wsem, 16)
                cx.W("tensor", wsem)
                if f == 0:
                    cx.W("tensor", "DV")
                last = (f == FC - 1) and (e_i != E - 1)
                for dd in range(DC):
                    emitter = cx.N if dd < DC - 1 else (lambda eng, fn: cx.I(eng, fn, "PE", 1))
                    emitter("tensor", lambda e, c, f=f, dd=dd, tsl=tsl, buf=buf, last=last: e.matmul(
                        c.st["p%d" % dd][:, :], c.st[buf][:, dd * 128:(dd + 1) * 128],
                        c.st["hprime"][:, f, tsl],
                        start=(f == 0), stop=last))
                pe_b[f] = plan.val("PE")
                cx.wd_last_pe[bi] = pe_b[f]
            cx.peB_prev = pe_b[FC - 1]
            if e_i == E - 1:
                for dd in range(DC):
                    emitter = cx.N if dd < DC - 1 else (lambda eng, fn: cx.I(eng, fn, "PE", 1))
                    emitter("tensor", lambda e, c, dd=dd, tsl=tsl: e.matmul(
                        c.st["p%d" % dd][:, :], c.st["dbn_sb"][0:8, dd * 128:(dd + 1) * 128],
                        c.st["combh"][0:8, tsl], start=False, stop=True))
            cx.W("vector", "PE")
            for dd in range(DC):
                cx.CH("vector", lambda e, c, dd=dd, tsl=tsl: e.tensor_tensor(
                    c.st["x_ch"][:, dd, tsl], c.st["x_ch"][:, dd, tsl], c.st["p%d" % dd][:, :], OP.add), "DV")


def emit_barrier(cx, bar_idx):
    st_needed = cx.plan.val("ST")
    cx.W("gpsimd", "ST", st_needed)
    b = bar_idx - 1

    def wr_flag(eng, cx_):
        r = cx_.regs["gpsimd"]
        dst = bass.AP(cx_.flg, r["rFmB%d" % b], [[4, 1], [1, 4]])
        return eng.dma_start(out=dst, in_=cx_.st["flagv_sb"][b:b + 1, 0:4])
    cx.I("gpsimd", wr_flag, "GPD", 16)
    cx.W("gpsimd", "GPD", cx.plan.val("GPD"))

    def poll(eng, cx_):
        r = cx_.regs["gpsimd"]
        eng.reg_load(r["rEx"], cx_.st["flagv_sb"][b:b + 1, 0:1])
        src = bass.AP(cx_.flg, r["rFpB%d" % b], [[4, 1], [1, 4]])
        eng.reg_mov(r["rC"], 1)
        with eng.While(r["rC"]):
            eng.dma_start(out=cx_.st["flgrd_sb"][0:1, 0:4], in_=src).then_inc(cx_.sems["PF"], 16)
            eng.reg_add(r["rW"], r["rW"], 16)
            eng.wait_ge(cx_.sems["PF"], r["rW"])
            eng.reg_load(r["rV"], cx_.st["flgrd_sb"][0:1, 0:1])
            eng.reg_alu(r["rC"], r["rV"], r["rEx"], OP.subtract)
    if not _LOCAL:
        cx.N("gpsimd", poll)
    cx.I("gpsimd", lambda e, c: e.nop(), "BAR", 1)


def emit_attention(cx):
    plan = cx.plan
    for hp in range(4):
        if hp > 0:
            cx.W("scalar", "ST", plan.val("ST"))
            cx.W("sync", "ST", plan.val("ST"))
        pe_kb = {}
        for kb in range(32):
            buf = "xnt_a" if kb % 2 == 0 else "xnt_b"
            if kb >= 2:
                cx.W("sync", "PE", pe_kb[kb - 2])
            ARP = 26 * 1024
            def f_load0(eng, c, kb=kb, buf=buf):
                src = bass.AP(c.xn1g, kb * 128, [[S, 128], [128 * S, DC], [1, 128]])
                dst = sub_ap(c.st[buf], 0, [[ARP, 128], [256, DC], [1, 128]])
                return eng.dma_start(out=dst, in_=src)
            lsem = "LDA" if kb % 2 == 0 else "LDB"
            cx.I("sync", f_load0, lsem, 16)
            def f_load1(eng, c, kb=kb, buf=buf):
                src = bass.AP(c.xn1g, 4096 + kb * 128, [[S, 128], [128 * S, DC], [1, 128]])
                dst = sub_ap(c.st[buf], 128, [[ARP, 128], [256, DC], [1, 128]])
                return eng.dma_start(out=dst, in_=src)
            cx.I("sync", f_load1, lsem, 16)
            cx.W("tensor", lsem)
            cx.W("tensor", "SC")
            cx.W("tensor", "DV")
            for dc in range(DC):
                cx.N("tensor", lambda e, c, dc=dc, hp=hp, buf=buf: e.matmul(
                    c.st["p0"][:, 0:256], c.st["wk_sb"][:, dc * 512 + hp * 128:dc * 512 + (hp + 1) * 128],
                    c.st[buf][:, dc * 256:(dc + 1) * 256],
                    start=(dc == 0), stop=(dc == DC - 1)))
            for tsub in range(2):
                for dc in range(DC):
                    emitter = cx.N if not (tsub == 1 and dc == DC - 1) else (lambda eng, fn: cx.I(eng, fn, "PE", 1))
                    emitter("tensor", lambda e, c, dc=dc, tsub=tsub, hp=hp, buf=buf: e.matmul(
                        c.st["p1"][:, tsub * 128:(tsub + 1) * 128],
                        c.st[buf][:, dc * 256 + tsub * 128:dc * 256 + (tsub + 1) * 128],
                        c.st["wv_sb"][:, dc * 512 + hp * 128:dc * 512 + (hp + 1) * 128],
                        start=(dc == 0), stop=(dc == DC - 1)))
            pe_kb[kb] = plan.val("PE")
            cx.W("scalar", "PE")
            cx.I("scalar", lambda e, c, kb=kb, hp=hp: e.activation(
                c.st["k_sb"][:, kb * 256:(kb + 1) * 256], c.st["p0"][:, 0:256], AF.Identity,
                bias=c.st["bqkv_sb"][:, hp * 3 + 1:hp * 3 + 2]), "SC", 1)
            cx.W("vector", "PE")
            cx.I("vector", lambda e, c, kb=kb: e.tensor_copy(
                c.st["v_sb"][:, kb * 256:(kb + 1) * 256], c.st["p1"][:, 0:256]), "DV", 1)
        pe_qb = {}
        for qb in range(16):
            buf = "xnt_a" if qb % 2 == 0 else "xnt_b"
            if qb >= 2:
                cx.W("sync", "PE", pe_qb[qb - 2])
            else:
                cx.W("sync", "PE", pe_kb[31 - (1 - qb)])
            def f_loadq(eng, c, qb=qb, buf=buf):
                src = bass.AP(c.xn1p, qb * 256, [[4096, 128], [128 * 4096, DC], [1, 256]])
                return eng.dma_start(out=c.st[buf][:, :], in_=src)
            lsem = "LDA" if qb % 2 == 0 else "LDB"
            cx.I("sync", f_loadq, lsem, 16)
            cx.W("tensor", lsem)
            cx.W("tensor", "SC")
            for dc in range(DC):
                emitter = cx.N if dc < DC - 1 else (lambda eng, fn: cx.I(eng, fn, "PE", 1))
                emitter("tensor", lambda e, c, dc=dc, hp=hp, buf=buf: e.matmul(
                    c.st["p0"][:, 0:256], c.st["wq_sb"][:, dc * 512 + hp * 128:dc * 512 + (hp + 1) * 128],
                    c.st[buf][:, dc * 256:(dc + 1) * 256],
                    start=(dc == 0), stop=(dc == DC - 1)))
            pe_qb[qb] = plan.val("PE")
            cx.W("scalar", "PE")
            cx.I("scalar", lambda e, c, qb=qb, hp=hp: e.activation(
                c.st["q_sb"][:, qb * 256:(qb + 1) * 256], c.st["p0"][:, 0:256], AF.Identity,
                bias=c.st["bqkv8_sb"][:, hp * 3:hp * 3 + 1], scale=0.125), "SC", 1)
        if _KATT == "proj":
            continue
        for cq in range(16):
            n_kt = 4 * cq + 4
            sc_after = {}
            dv_after_kt = {}
            pe_scores = {}
            sc_boundary = cx.plan.val("SC")
            dv_boundary = cx.plan.val("DV")

            def emit_scores(kt, cq=cq):
                plg0 = "p2" if kt % 2 == 0 else "p3"
                plg1 = "p0" if kt % 2 == 0 else "p1"
                if kt >= 2:
                    cx.W("tensor", "SC", sc_after[kt - 2])
                else:
                    cx.W("tensor", "SC", sc_boundary)
                    if cq == 0:
                        cx.W("tensor", "DV", dv_boundary)
                for hh in range(2):
                    plg = plg0 if hh == 0 else plg1
                    emitter = cx.N if hh == 0 else (lambda eng, fn: cx.I(eng, fn, "PE", 1))
                    emitter("tensor", lambda e, c, kt=kt, cq=cq, hh=hh, plg=plg: e.matmul(
                        c.st[plg][:, 0:256],
                        c.st["k_sb"][hh * 64:(hh + 1) * 64, kt * 128:(kt + 1) * 128],
                        c.st["q_sb"][hh * 64:(hh + 1) * 64, cq * 256:(cq + 1) * 256],
                        start=True, stop=True, tile_position=(hh * 64, 0)))
                pe_scores[kt] = cx.plan.val("PE")

            def emit_exp_mask(kt, cq=cq):
                plg0 = "p2" if kt % 2 == 0 else "p3"
                plg1 = "p0" if kt % 2 == 0 else "p1"
                pt = "pt_a" if kt % 2 == 0 else "pt_b"
                cx.W("scalar", "PE", pe_scores[kt])
                cx.I("scalar", lambda e, c, plg0=plg0, pt=pt: e.activation(
                    c.st[pt][:, 0:256], c.st[plg0][:, 0:256], AF.Exp), "SC", 1)
                cx.I("scalar", lambda e, c, plg1=plg1, pt=pt: e.activation(
                    c.st[pt][:, 256:512], c.st[plg1][:, 0:256], AF.Exp), "SC", 1)
                sc_after[kt] = cx.plan.val("SC")
                r = kt - 4 * cq
                if r >= 0:
                    cx.W("vector", "SC", sc_after[kt])
                    cx.I("vector", lambda e, c, pt=pt, r=r: e.tensor_tensor(
                        c.st[pt][:, :], c.st[pt][:, :], c.st["mask_sb"][:, r * 512:(r + 1) * 512], OP.mult), "DV", 1)
                dv_after_kt[kt] = cx.plan.val("DV")

            def emit_av(kt, cq=cq, n_kt=n_kt):
                pt = "pt_a" if kt % 2 == 0 else "pt_b"
                r = kt - 4 * cq
                if r >= 0:
                    cx.W("tensor", "DV", dv_after_kt[kt])
                else:
                    cx.W("tensor", "SC", sc_after[kt])
                for hh in range(2):
                    pav = "p4" if hh == 0 else "p5"
                    cx.N("tensor", lambda e, c, kt=kt, hh=hh, pt=pt, n_kt=n_kt, pav=pav: e.matmul(
                        c.st[pav][hh * 64:(hh + 1) * 64, 0:256],
                        c.st["v_sb"][:, kt * 128 + hh * 64:kt * 128 + (hh + 1) * 64],
                        c.st[pt][:, hh * 256:(hh + 1) * 256],
                        start=(kt == 0), stop=(kt == n_kt - 1), tile_position=(0, hh * 64)))
                cx.I("tensor", lambda e, c, kt=kt, pt=pt, n_kt=n_kt: e.matmul(
                    c.st["p6"][0:1, 0:512],
                    c.st["onesc_sb"][:, 0:1],
                    c.st[pt][:, 0:512],
                    start=(kt == 0), stop=(kt == n_kt - 1)), "PE", 1)

            if _KATT == "lg":
                for kt in range(n_kt):
                    emit_scores(kt)
                    emit_exp_mask(kt)
                continue
            emit_scores(0)
            emit_exp_mask(0)
            for kt in range(n_kt):
                if kt + 1 < n_kt:
                    emit_scores(kt + 1)
                    emit_exp_mask(kt + 1)
                emit_av(kt)
            if _KATT == "av":
                continue
            cx.W("vector", "PE")
            cx.CH("vector", lambda e, c: e.tensor_copy(c.st["rden"][0:1, :], c.st["p6"][0:1, :]), "DV")
            cx.CH("vector", lambda e, c: e.reciprocal(c.st["rden"][0:1, :], c.st["rden"][0:1, :]), "DV")
            cx.CH("vector", lambda e, c: e.tensor_copy(c.st["rdenb"][0:1, :], c.st["rden"][0:1, :]), "DV")
            cx.W("tensor", "DV")
            for hh in range(2):
                emitter = cx.N if hh == 0 else (lambda eng, fn: cx.I(eng, fn, "PE", 1))
                emitter("tensor", lambda e, c, hh=hh: e.matmul(
                    c.st["p7"][hh * 64:(hh + 1) * 64, 0:256],
                    c.st["onesl_sb"][0:1, 0:64],
                    c.st["rdenb"][0:1, hh * 256:(hh + 1) * 256],
                    start=True, stop=True, tile_position=(0, hh * 64)))
            cx.W("vector", "PE")
            cx.CH("vector", lambda e, c: e.tensor_copy(c.st["rbc"][:, :], c.st["p7"][:, 0:256]), "DV")
            cx.CH("vector", lambda e, c: e.tensor_tensor(c.st["avn"][0:64, :], c.st["p4"][0:64, 0:256], c.st["rbc"][0:64, :], OP.mult), "DV")
            cx.CH("vector", lambda e, c: e.tensor_tensor(c.st["avn"][64:128, :], c.st["p5"][64:128, 0:256], c.st["rbc"][64:128, :], OP.mult), "DV")
            cx.W("scalar", "DV")
            cx.I("scalar", lambda e, c, hp=hp, cq=cq: e.activation(
                c.st["att_hp"][:, cq * 256:(cq + 1) * 256], c.st["avn"][:, :], AF.Identity,
                bias=c.st["bqkv_sb"][:, hp * 3 + 2:hp * 3 + 3]), "SC", 1)
        cx.W("sync", "SC")
        def f_store(eng, c, hp=hp):
            r = c.regs["sync"]
            eng.reg_alu(r["rU"], r["rA"], hp * 128 * 4096, OP.add)
            dst = bass.AP(c.attnx, r["rU"], [[4096, 128], [1, 4096]])
            return eng.dma_start(out=dst, in_=c.st["att_hp"][:, :])
        cx.I("sync", f_store, "ST", 16)


def emit_outproj_ffn2(cx):
    for c2 in range(NCH2):
        for ocs in range(2):
            oc = c2 * 2 + ocs
            cx.W("sync", "PE")
            cx.W("sync", "ST", cx.plan.val("ST"))
            def f_la(eng, c, oc=oc):
                r = c.regs["sync"]
                eng.reg_alu(r["rU"], r["rO"], oc * 512, OP.add)
                src = bass.AP(c.attnx, r["rU"], [[4096, 128], [128 * 4096, DC], [1, 512]])
                return eng.dma_start(out=c.st["aall"][:, :, :], in_=src)
            cx.I("sync", f_la, "LDAL", 16)
            def f_lx(eng, c, oc=oc, ocs=ocs):
                src = bass.AP(c.x1park, oc * 512, [[QTOK, 128], [128 * QTOK, DC], [1, 512]])
                return eng.dma_start(out=c.st["x_ch"][:, :, ocs * 512:(ocs + 1) * 512], in_=src)
            cx.I("sync", f_lx, "LDX", 16)
            cx.W("tensor", "LDAL")
            cx.W("vector", "LDX")
            for dd in range(DC):
                ps = "p%d" % (dd % 2)
                cx.W("tensor", "SC")
                for cc in range(DC):
                    emitter = cx.N if cc < DC - 1 else (lambda eng, fn: cx.I(eng, fn, "PE", 1))
                    emitter("tensor", lambda e, c, dd=dd, cc=cc, ps=ps: e.matmul(
                        c.st[ps][:, :], c.st["wo_sb"][:, cc * 1024 + dd * 128:cc * 1024 + (dd + 1) * 128],
                        c.st["aall"][:, cc, :], start=(cc == 0), stop=(cc == DC - 1)))
                cx.W("scalar", "PE")
                cx.W("scalar", "DV")
                cx.I("scalar", lambda e, c, dd=dd, ps=ps: e.activation(
                    c.st["x2t"][:, :], c.st[ps][:, :], AF.Identity,
                    bias=c.st["bout_sb"][:, dd:dd + 1]), "SC", 1)
                cx.W("vector", "SC")
                cx.I("vector", lambda e, c, dd=dd, ocs=ocs: e.tensor_tensor(
                    c.st["x_ch"][:, dd, ocs * 512:(ocs + 1) * 512],
                    c.st["x_ch"][:, dd, ocs * 512:(ocs + 1) * 512],
                    c.st["x2t"][:, :], OP.add), "DV", 1)
        emit_ln(cx, "x_ch", "xn_ch")
        emit_gate(cx)
        emit_moe(cx)
        cx.W("sync", "DV")
        def f_out(eng, c, c2=c2):
            dst = c.dr["out"].ap()[:, c2 * TC:(c2 + 1) * TC].rearrange("(dc p) t -> p dc t", p=128)
            return eng.dma_start(out=dst, in_=c.st["x_ch"][:, :, :])
        cx.I("sync", f_out, "ST", 16)


def emit_ffn1(cx):
    for ch in range(NCH1):
        cx.W("sync", "DV")
        cx.W("sync", "ST", cx.plan.val("ST"))
        def f_lx(eng, c, ch=ch):
            src = bass.AP(c.dr["xin"], ch * TC, [[TOKS_CORE, 128], [128 * TOKS_CORE, DC], [1, TC]])
            return eng.dma_start(out=c.st["x_ch"][:, :, :], in_=src)
        cx.I("sync", f_lx, "LDX", 16)
        ld_now = cx.plan.val("LDX")
        st_now = cx.plan.val("ST")
        for engn in ("tensor", "vector", "scalar"):
            cx.W(engn, "LDX", ld_now)
        cx.W("vector", "ST", st_now)
        emit_ln(cx, "x_ch", "xn_ch")
        emit_gate(cx)
        emit_moe(cx)
        emit_ln(cx, "x_ch", "xn_ch")
        dv_now = cx.plan.val("DV")
        cx.W("sync", "DV", dv_now)
        if ch < 2:
            def f_park(eng, c, ch=ch):
                dst = bass.AP(c.x1park, ch * TC, [[QTOK, 128], [128 * QTOK, DC], [1, TC]])
                return eng.dma_start(out=dst, in_=c.st["x_ch"][:, :, :])
            cx.I("sync", f_park, "ST", 16)

            def f_packed(eng, c, ch=ch):
                r = c.regs["sync"]
                eng.reg_alu(r["rT"], r["rP"], ch * TC, OP.add)
                dst = bass.AP(c.xn1p, r["rT"], [[4096, 128], [128 * 4096, DC], [1, TC]])
                return eng.dma_start(out=dst, in_=c.st["xn_ch"][:, :, :])
            cx.I("sync", f_packed, "ST", 16)
        def f_g(eng, c, ch=ch):
            r = c.regs["sync"]
            eng.reg_load(r["rT"], c.st["offs_sb"][0:1, 6 + ch:7 + ch])
            dst = bass.AP(c.xn1g, r["rT"], [[S, 128], [128 * S, DC], [1, TC]])
            return eng.dma_start(out=dst, in_=c.st["xn_ch"][:, :, :])
        cx.I("sync", f_g, "ST", 16)



def build_nc():
    nc = bass.Bass("TRN2", target_bir_lowering=False, debug=False,
                   detect_race_conditions=not _NORACE)
    plan = Plan()
    cx = Ctx(nc, plan)

    def din(name, shape, dt=F32):
        cx.dr[name] = nc.dram_tensor(name, list(shape), dt, kind="ExternalInput")

    din("xin", (D, TOKS_CORE))
    din("wq", (D, 512)); din("wk", (D, 512)); din("wv", (D, 512))
    din("bqkv", (128, 12))
    din("wout", (D, D))
    din("bout", (128, DC))
    din("gwT", (D, E))
    din("gb", (E, 1))
    din("upgp", (E, FC, 128, DC * 256))
    din("dwn", (E, FC, 128, D))
    din("upb", (128, E * FC)); din("gpb", (128, E * FC))
    din("dbn", (E, D))
    din("maskc", (128, 4 * 512), BF16)
    din("iota8", (128, E))
    din("ident", (128, 128))
    din("onesc", (128, 1), BF16)
    din("onesl", (1, 128), BF16)
    din("sel8", (E, E * 128), BF16)
    din("offs", (1, 64), I32)
    din("flagv", (2, 16), I32)
    cx.dr["out"] = nc.dram_tensor("out", [D, QTOK], F32, kind="ExternalOutput")

    _spc = "Local" if _LOCAL else "Shared"
    cx.xn1g = nc.dram_tensor("xn1g", [D, S], BF16, kind="Internal", addr_space=_spc)
    cx.xn1p = nc.dram_tensor("xn1p", [D, 2 * QTOK], BF16, kind="Internal", addr_space=_spc)
    cx.attnx = nc.dram_tensor("attnx", [D, 2 * QTOK], BF16, kind="Internal", addr_space=_spc)
    cx.flg = nc.dram_tensor("flg", [2, 16], I32, kind="Internal", addr_space=_spc)
    cx.x1park = nc.dram_tensor("x1park", [D, QTOK], F32, kind="Internal")

    for s in ("LD", "LDX", "LDA", "LDB", "LDAL", "WGT", "WGTA", "WGTB", "WGTC",
              "WGTDA", "WGTDB", "WGTDC", "WGTDD", "PE", "DV", "SC", "ST", "BAR", "GPD", "PF"):
        plan.sem(s)

    for nm_d, nm_s in [("bqkv_sb", "bqkv"), ("bout_sb", "bout"), ("gb_sb", "gb"),
                       ("upb_sb", "upb"), ("gpb_sb", "gpb"),
                       ("mask_sb", "maskc"), ("iota_sb", "iota8"),
                       ("ident_sb", "ident"), ("onesc_sb", "onesc"),
                       ("onesl_sb", "onesl"), ("sel8_sb", "sel8"),
                       ("offs_sb", "offs"), ("flagv_sb", "flagv")]:
        cx.I("sync", lambda e, c, nm_d=nm_d, nm_s=nm_s: e.dma_start(
            out=c.st[nm_d][:, :], in_=c.dr[nm_s].ap()), "LD", 16)
    cx.I("gpsimd", lambda e, c: e.dma_start(out=c.st["gw_sb"][:, :].rearrange("p (dc e) -> p dc e", dc=DC), in_=c.dr["gwT"].ap().rearrange("(dc p) e -> p dc e", p=128)), "WGT", 16)
    cx.I("gpsimd", lambda e, c: e.dma_start(out=c.st["wq_sb"][:, :].rearrange("p (dc h) -> p dc h", dc=DC), in_=c.dr["wq"].ap().rearrange("(dc p) h -> p dc h", p=128)), "WGT", 16)
    cx.I("gpsimd", lambda e, c: e.dma_start(out=c.st["wk_sb"][:, :].rearrange("p (dc h) -> p dc h", dc=DC), in_=c.dr["wk"].ap().rearrange("(dc p) h -> p dc h", p=128)), "WGT", 16)
    cx.I("gpsimd", lambda e, c: e.dma_start(out=c.st["wv_sb"][:, :].rearrange("p (dc h) -> p dc h", dc=DC), in_=c.dr["wv"].ap().rearrange("(dc p) h -> p dc h", p=128)), "WGT", 16)
    cx.I("gpsimd", lambda e, c: e.dma_start(out=c.st["wo_sb"][:, :].rearrange("p (dc d) -> p dc d", dc=DC), in_=c.dr["wout"].ap().rearrange("(dc p) d -> p dc d", p=128)), "WGT", 16)
    cx.I("gpsimd", lambda e, c: e.dma_start(out=c.st["dbn_sb"][0:8, :], in_=c.dr["dbn"].ap()), "WGT", 16)

    ld_all = plan.val("LD")
    wgt_all = plan.val("WGT")
    for engn in ENGINES:
        cx.W(engn, "LD", ld_all)
        cx.W(engn, "WGT", wgt_all)

    cx.CH("vector", lambda e, c: e.tensor_scalar(
        c.st["bqkv8_sb"][:, :], c.st["bqkv_sb"][:, :], 0.125, None, OP.mult), "DV")

    def f_regs_sync(eng, c):
        r = c.regs["sync"]
        eng.reg_load(r["rP"], c.st["offs_sb"][0:1, 1:2])
        eng.reg_load(r["rA"], c.st["offs_sb"][0:1, 2:3])
        eng.reg_load(r["rO"], c.st["offs_sb"][0:1, 3:4])
    cx.N("sync", f_regs_sync)

    def f_regs_gp(eng, c):
        r = c.regs["gpsimd"]
        eng.reg_load(r["rFmB0"], c.st["offs_sb"][0:1, 4:5])
        eng.reg_load(r["rFpB0"], c.st["offs_sb"][0:1, 5:6])
        eng.reg_alu(r["rFmB1"], r["rFmB0"], 4, OP.add)
        eng.reg_alu(r["rFpB1"], r["rFpB0"], 4, OP.add)
        eng.reg_mov(r["rW"], 0)
    cx.N("gpsimd", f_regs_gp)

    if _STAGE == 0:
        def f_lx0(eng, c):
            src_ = bass.AP(c.dr["xin"], 0, [[TOKS_CORE, 128], [128 * TOKS_CORE, DC], [1, TC]])
            return eng.dma_start(out=c.st["x_ch"][:, :, :], in_=src_)
        cx.I("sync", f_lx0, "LDX", 16)
        cx.W("sync", "LDX", plan.val("LDX"))
        def f_o0(eng, c):
            dst = c.dr["out"].ap()[:, 0:TC].rearrange("(dc p) t -> p dc t", p=128)
            return eng.dma_start(out=dst, in_=c.st["x_ch"][:, :, :])
        cx.I("sync", f_o0, "ST", 16)
        cx.W("sync", "ST", plan.val("ST"))
    if _STAGE >= 1:
        emit_ffn1(cx)
    if _STAGE >= 2:
        emit_barrier(cx, 1)
        bar1 = plan.val("BAR")
        for engn in ("tensor", "vector", "scalar", "sync"):
            cx.W(engn, "BAR", bar1)
        if _KATT != "none":
            emit_attention(cx)
    if _STAGE >= 3:
        emit_barrier(cx, 2)
        bar2 = plan.val("BAR")
        for engn in ("tensor", "vector", "scalar", "sync"):
            cx.W(engn, "BAR", bar2)
        emit_outproj_ffn2(cx)
    if _STAGE == 1:
        cx.W("sync", "ST", plan.val("ST"))
        def f_dbg(eng, c):
            src_ = bass.AP(c.x1park, 0, [[QTOK, 128], [128 * QTOK, DC], [1, QTOK]])
            dst = c.dr["out"].ap().rearrange("(dc p) t -> p dc t", p=128)
            return eng.dma_start(out=dst, in_=src_)
        cx.I("sync", f_dbg, "ST", 16)
    if _STAGE == 2:
        cx.W("gpsimd", "BAR", plan.val("BAR"))
        cx.W("gpsimd", "ST", plan.val("ST"))
        def f_dbg2(eng, c):
            r = c.regs["gpsimd"]
            eng.reg_load(r["rEx"], c.st["offs_sb"][0:1, 3:4])
            src_ = bass.AP(c.attnx, r["rEx"], [[4096, 128], [128 * 4096, DC], [1, QTOK]])
            dst = c.dr["out"].ap().rearrange("(dc p) t -> p dc t", p=128)
            return eng.dma_start(out=dst, in_=src_)
        cx.I("gpsimd", f_dbg2, "GPD", 16)
        cx.W("gpsimd", "GPD", plan.val("GPD"))
    cx.W("sync", "ST", plan.val("ST"))

    sbufs = [
        ("x_ch", [128, DC, TC], F32),
        ("xn_ch", [128, DC, TC], BF16),
        ("xsq_s", [128, TC], BF16),
        ("xb_s", [128, TC], BF16),
        ("arena", [128, 26 * 1024], BF16),
        ("wu_a", [128, DC * 256], BF16), ("wu_b", [128, DC * 256], BF16),
        ("wu_c", [128, DC * 256], BF16),
        ("wd_a", [128, D], BF16), ("wd_b", [128, D], BF16),
        ("wd_c", [128, D], BF16), ("wd_d", [128, D], BF16),
        ("wq_sb", [128, DC * 512], BF16), ("wk_sb", [128, DC * 512], BF16),
        ("wv_sb", [128, DC * 512], BF16),
        ("wo_sb", [128, DC * 1024], BF16),
        ("bqkv_sb", [128, 12], F32),
        ("bqkv8_sb", [128, 12], F32),
        ("bout_sb", [128, DC], F32),
        ("gw_sb", [128, DC * 8], BF16),
        ("gb_sb", [E, 1], F32),
        ("upb_sb", [128, E * FC], F32), ("gpb_sb", [128, E * FC], F32),
        ("dbn_sb", [E, D], BF16),
        ("mask_sb", [128, 4 * 512], BF16),
        ("iota_sb", [128, E], F32),
        ("ident_sb", [128, 128], F32),
        ("onesc_sb", [128, 1], BF16),
        ("onesl_sb", [1, 128], BF16),
        ("sel8_sb", [E, E * 128], BF16),
        ("offs_sb", [1, 64], I32),
        ("flagv_sb", [2, 16], I32),
        ("flgrd_sb", [1, 16], I32),
        ("gsc", [E, TC], F32),
        ("gtm", [128, 8 * E], F32),
        ("gm1", [128, 8], F32), ("gm2", [128, 8], F32),
        ("gi", [128, 8], F32),
        ("gtmp", [128, E], F32),
        ("gmask1", [128, 8 * E], F32), ("gmask2", [128, 8 * E], F32),
        ("gd21", [128, 8], F32),
        ("gw1", [128, 8], F32), ("gw2", [128, 8], F32),
        ("gcomb", [128, 8 * E], F32),
        ("combh", [E, TC], BF16),
        ("wbc", [128, TC], BF16),
        ("lnA", [1, 1024], F32), ("lnB", [1, 1024], F32),
        ("lnb", [1, 1024], BF16),
        ("lnt", [128, 512], F32),
        ("upev", [128, TC], F32),
        ("gpev", [128, TC], F32),
        ("hsil", [128, TC], BF16),
    ]
    psums = [("p%d" % i, [128, 512], F32) for i in range(8)]

    with ExitStack() as es:
        for nm, shape, dt in sbufs:
            cx.st[nm] = es.enter_context(nc.sbuf_tensor(nm, shape, dt))
        cx.st["x2t"] = cx.st["lnt"][:, :]
        ar = cx.st["arena"]
        cx.st["hprime"] = ar[:, :].rearrange("p (f t) -> p f t", f=FC)
        cx.st["k_sb"] = ar[:, 0:8192]
        cx.st["v_sb"] = ar[:, 8192:16384]
        cx.st["q_sb"] = ar[:, 16384:20480]
        cx.st["xnt_a"] = ar[:, 20480:22528]
        cx.st["xnt_b"] = ar[:, 22528:24576]
        cx.st["pt_a"] = ar[:, 24576:25088]
        cx.st["pt_b"] = ar[:, 25088:25600]
        cx.st["aall"] = ar[:, 0:4096].rearrange("p (dc t) -> p dc t", dc=DC)
        cx.st["att_hp"] = ar[:, 20480:24576]
        cx.st["lnC"] = cx.st["gpev"][0:1, 0:1024]
        cx.st["rden"] = cx.st["lnA"][0:1, 0:512]
        cx.st["rdenb"] = cx.st["lnb"][0:1, 0:512]
        cx.st["rbc"] = cx.st["lnt"][:, 0:256]
        cx.st["avn"] = cx.st["lnt"][:, 256:512]
        for nm, shape, dt in psums:
            cx.st[nm] = es.enter_context(nc.psum_tensor(nm, shape, dt))
        for sname in plan.sem_names:
            cx.sems[sname] = es.enter_context(nc.semaphore(sname))
        block = es.enter_context(nc.Block())

        @block.sync
        def _(eng):
            with ExitStack() as res:
                cx.regs["sync"] = {k: res.enter_context(eng.register(k))
                                   for k in ("rP", "rA", "rO", "rT", "rU")}
                for fn in plan.ops["sync"]:
                    fn(eng, cx)

        @block.gpsimd
        def _(eng):
            with ExitStack() as res:
                cx.regs["gpsimd"] = {k: res.enter_context(eng.register(k))
                                     for k in ("rFmB0", "rFpB0", "rFmB1", "rFpB1",
                                               "rEx", "rV", "rC", "rW")}
                for fn in plan.ops["gpsimd"]:
                    fn(eng, cx)

        @block.tensor
        def _(eng):
            for fn in plan.ops["tensor"]:
                fn(eng, cx)

        @block.vector
        def _(eng):
            for fn in plan.ops["vector"]:
                fn(eng, cx)

        @block.scalar
        def _(eng):
            for fn in plan.ops["scalar"]:
                fn(eng, cx)

    return nc



def _stage_core(core, inp, nonce, shared):
    g = core // 2
    batch = g // 2
    a = g % 2
    p = core % 2

    x = shared["x"]
    m = {}
    tok_idx = []
    for i in range(32):
        if i < 16:
            gt = 2 * (p * 16 + i) + a
        else:
            gt = 2 * (p * 16 + (i - 16)) + (1 - a)
        tok_idx.extend(range(gt * 128, (gt + 1) * 128))
    m["xin"] = np.ascontiguousarray(x[batch][tok_idx, :].T)
    m["wq"] = shared["wq"][p]
    m["wk"] = shared["wk"][p]
    m["wv"] = shared["wv"][p]
    m["bqkv"] = shared["bqkv"][p]
    m["wout"] = shared["wout"]
    m["bout"] = shared["bout"]
    m["gwT"] = shared["gwT"]
    m["gb"] = shared["gb"]
    m["upgp"] = shared["upgp"]
    m["dwn"] = shared["dwn"]
    m["upb"] = shared["upb"]
    m["gpb"] = shared["gpb"]
    m["dbn"] = shared["dbn"]
    m["maskc"] = shared["maskc"][a]
    m["iota8"] = shared["iota8"]
    m["ident"] = shared["ident"]
    m["onesc"] = shared["onesc"]
    m["onesl"] = shared["onesl"]
    m["sel8"] = shared["sel8"]

    offs = np.zeros((1, 64), np.int32)
    offs[0, 1] = p * 2048
    offs[0, 2] = p * 512 * 4096
    offs[0, 3] = p * 2048
    offs[0, 4] = p * 16
    offs[0, 5] = (1 - p) * 16
    for ch in range(4):
        if ch < 2:
            side = a
            tb = p * 16 + ch * 8
        else:
            side = 1 - a
            tb = p * 16 + (ch - 2) * 8
        offs[0, 6 + ch] = side * 4096 + tb * 128
    m["offs"] = offs
    flagv = np.zeros((2, 16), np.int32)
    flagv[0, :] = nonce + 1
    flagv[1, :] = nonce + 2
    m["flagv"] = flagv
    return m


def _stage_shared(inp):
    sh = {}
    sh["x"] = np.asarray(inp["L0_attn_x"], np.float32)
    W_qkv = np.asarray(inp["L0_attn_W_qkv"], np.float32)
    b_qkv = np.asarray(inp["L0_attn_b_qkv"], np.float32)
    sh["wout"] = np.ascontiguousarray(np.asarray(inp["L0_attn_W_out"], np.float32))
    b_out = np.asarray(inp["L0_attn_b_out"], np.float32)
    sh["bout"] = np.ascontiguousarray(b_out.reshape(DC, 128).T)
    gate_w = np.asarray(inp["L0_ffn_gate_w"], np.float32)
    sh["gwT"] = np.ascontiguousarray(gate_w.T)
    sh["gb"] = np.ascontiguousarray(np.asarray(inp["L0_ffn_gate_b"], np.float32).reshape(E, 1))
    up_w = np.asarray(inp["L0_ffn_up_w"], np.float32)
    up_b = np.asarray(inp["L0_ffn_up_b"], np.float32)
    gp_w = np.asarray(inp["L0_ffn_gp_w"], np.float32)
    gp_b = np.asarray(inp["L0_ffn_gp_b"], np.float32)
    down_w = np.asarray(inp["L0_ffn_down_w"], np.float32)
    down_b = np.asarray(inp["L0_ffn_down_b"], np.float32)

    sh["wq"], sh["wk"], sh["wv"], sh["bqkv"] = [], [], [], []
    for p in range(2):
        heads = list(range(p * 8, p * 8 + 8))
        qcols, kcols, vcols = [], [], []
        for h in heads:
            qcols += list(range(h * 192, h * 192 + 64))
            kcols += list(range(h * 192 + 64, h * 192 + 128))
            vcols += list(range(h * 192 + 128, h * 192 + 192))
        sh["wq"].append(np.ascontiguousarray(W_qkv[:, qcols]))
        sh["wk"].append(np.ascontiguousarray(W_qkv[:, kcols]))
        sh["wv"].append(np.ascontiguousarray(W_qkv[:, vcols]))
        bq = b_qkv[qcols].reshape(4, 2, 64)
        bk = b_qkv[kcols].reshape(4, 2, 64)
        bv = b_qkv[vcols].reshape(4, 2, 64)
        bqkv = np.zeros((128, 12), np.float32)
        for hp in range(4):
            bqkv[:, hp * 3 + 0] = bq[hp].reshape(128)
            bqkv[:, hp * 3 + 1] = bk[hp].reshape(128)
            bqkv[:, hp * 3 + 2] = bv[hp].reshape(128)
        sh["bqkv"].append(bqkv)

    upgp = np.zeros((E, FC, D, 256), np.float32)
    upT = np.transpose(up_w, (0, 2, 1))
    gpT = np.transpose(gp_w, (0, 2, 1))
    for fch in range(FC):
        f0 = fch * 128
        f1 = min(f0 + 128, F)
        w = f1 - f0
        upgp[:, fch, :, 0:w] = upT[:, :, f0:f1]
        upgp[:, fch, :, 128:128 + w] = gpT[:, :, f0:f1]
    sh["upgp"] = np.ascontiguousarray(
        upgp.reshape(E, FC, DC, 128, 256).transpose(0, 1, 3, 2, 4).reshape(E, FC, 128, DC * 256))
    dwn = np.zeros((E, FC, 128, D), np.float32)
    dwnT = np.transpose(down_w, (0, 2, 1))
    for fch in range(FC):
        f0 = fch * 128
        f1 = min(f0 + 128, F)
        dwn[:, fch, 0:f1 - f0, :] = dwnT[:, f0:f1, :]
    sh["dwn"] = dwn

    def bias_grid(b):
        g_ = np.zeros((128, E * FC), np.float32)
        for e_i in range(E):
            for fch in range(FC):
                f0 = fch * 128
                f1 = min(f0 + 128, F)
                g_[0:f1 - f0, e_i * FC + fch] = b[e_i, f0:f1]
        return g_
    sh["upb"] = bias_grid(up_b)
    sh["gpb"] = bias_grid(gp_b)
    sh["dbn"] = np.ascontiguousarray(down_b)

    sh["maskc"] = []
    kk = np.arange(128)[:, None]
    qq = np.arange(128)[None, :]
    for a in range(2):
        maskc = np.zeros((128, 4 * 512), np.float32)
        for r in range(4):
            mlo = (r * 128 + kk <= a * 128 + qq).astype(np.float32)
            mhi = (r * 128 + kk <= (2 + a) * 128 + qq).astype(np.float32)
            blk = np.concatenate([mlo, mhi], axis=1)
            maskc[:, r * 512:r * 512 + 256] = blk
            maskc[:, r * 512 + 256:(r + 1) * 512] = blk
        sh["maskc"].append(maskc.astype(bfdt))

    sh["iota8"] = np.broadcast_to(np.arange(E, dtype=np.float32), (128, E)).copy()
    sh["ident"] = np.eye(128, dtype=np.float32)
    sh["onesc"] = np.ones((128, 1), bfdt)
    sh["onesl"] = np.ones((1, 128), bfdt)
    sel8 = np.zeros((E, E * 128), np.float32)
    for e_i in range(E):
        sel8[e_i, e_i * 128:(e_i + 1) * 128] = 1.0
    sh["sel8"] = sel8.astype(bfdt)
    return sh


def kernel(**inputs):
    nonce = int(np.random.default_rng().integers(1, 1 << 24)) * 4
    nc = build_nc()
    shared = _stage_shared(inputs)
    in_maps = [_stage_core(c, inputs, nonce, shared) for c in range(8)]
    res = run_bass_kernel_spmd(nc, in_maps, core_ids=list(range(8)))

    out = np.zeros((2, S, D), np.float32)
    for core in range(8):
        g = core // 2
        batch = g // 2
        a = g % 2
        p = core % 2
        y = res.results[core]["out"]
        for jj in range(16):
            j = p * 16 + jj
            gt = 2 * j + a
            out[batch, gt * 128:(gt + 1) * 128, :] = y[:, jj * 128:(jj + 1) * 128].T
    return out
